# version 09
# baseline (speedup 1.0000x reference)
import numpy as np
import concourse.bass as bass
import concourse.mybir as mybir
from concourse.bass_utils import run_bass_kernel_spmd

F32 = mybir.dt.float32
F16 = mybir.dt.float16
BF16 = mybir.dt.bfloat16
AF = mybir.ActivationFunctionType
ALU = mybir.AluOpType
AX = mybir.AxisListType

T = 2048
DM = 1024
DI = 2048
S = 16
NF = 2816
TC = 512
NCH = T // TC
KC = DM // 128
DIC = DI // 128
NFC = NF // 128

CST = [-1.0, 1.0, -0.25, -0.75, -1.25, -1.75, -2.5, -3.5, -5.0]
(IC_NEG1, IC_1, IC_M25, IC_M75, IC_M125, IC_M175,
 IC_M25X, IC_M35X, IC_M5X) = range(9)

ENGS = ("sync", "tensor", "vector", "scalar", "gpsimd")


def build(stage="full", dbg=()):
    nc = bass.Bass("TRN2", target_bir_lowering=False)

    def dram(name, shape, dtype):
        kind = "ExternalOutput" if name in dbg else "Internal"
        return nc.dram_tensor(name, shape, dtype, kind=kind).ap()

    x_d = nc.dram_tensor("x", [T, DM], F32, kind="ExternalInput").ap()
    win_d = nc.dram_tensor("w_in", [DM, 2 * DI], F32, kind="ExternalInput").ap()
    wout_d = nc.dram_tensor("w_out", [DI, DM], F32, kind="ExternalInput").ap()
    wg_d = nc.dram_tensor("w_gate", [NF, DM], F32, kind="ExternalInput").ap()
    wu_d = nc.dram_tensor("w_up", [NF, DM], F32, kind="ExternalInput").ap()
    wd_d = nc.dram_tensor("w_down", [DM, NF], F32, kind="ExternalInput").ap()
    alog_d = nc.dram_tensor("a_log", [128, DIC * S], F32, kind="ExternalInput").ap()
    db_d = nc.dram_tensor("dt_bias", [128, DIC], F32, kind="ExternalInput").ap()
    dp_d = nc.dram_tensor("d_param", [128, DIC], F32, kind="ExternalInput").ap()
    cb_d = nc.dram_tensor("conv_b", [128, DIC], F32, kind="ExternalInput").ap()
    cw_d = nc.dram_tensor("conv_w", [128, DIC * 4], F32, kind="ExternalInput").ap()
    wx_d = nc.dram_tensor("w_x", [128, DIC * 33], F32, kind="ExternalInput").ap()
    cst_d = nc.dram_tensor("cst", [128, len(CST)], F32, kind="ExternalInput").ap()
    idn_d = nc.dram_tensor("idn", [128, 128], F32, kind="ExternalInput").ap()
    out_d = nc.dram_tensor("out", [T, DM], F32, kind="ExternalOutput").ap()

    winh_d = dram("s_winh", [DM, 2 * DI], F16)
    winl_d = dram("s_winl", [DM, 2 * DI], F16)
    wouth_d = dram("s_wouth", [DI, DM], F16)
    woutl_d = dram("s_woutl", [DI, DM], F16)
    qwga_d = dram("s_qwga", [NF, DM], F16)
    qwgb_d = dram("s_qwgb", [NF, DM], F16)
    qwua_d = dram("s_qwua", [NF, DM], F16)
    qwub_d = dram("s_qwub", [NF, DM], F16)
    qwda_d = dram("s_qwda", [DM, NF], F16)
    qwdb_d = dram("s_qwdb", [DM, NF], F16)
    xnh_d = dram("s_xnh", [T, DM], F16)
    xnl_d = dram("s_xnl", [T, DM], F16)
    qxa_d = dram("s_qxa", [T, DM], F16)
    qxb_d = dram("s_qxb", [T, DM], F16)
    ssm_d = dram("s_ssm", [33, T], F32)
    qha_d = dram("s_qha", [T, NF], F16)
    qhb_d = dram("s_qhb", [T, NF], F16)
    ffn_d = dram("s_ffn", [DM, T], F32)
    xb_dbg = dram("s_xb", [DI, T], F32)
    xc_dbg = dram("s_xc", [DI, T], F32)
    dt_dbg = dram("s_dt", [DI, T], F32)
    y_dbg = dram("s_y", [DI, T], F32)

    cst_s = nc.alloc_sbuf_tensor("cst_s", [128, len(CST)], F32).ap()
    idn_s = nc.alloc_sbuf_tensor("idn_s", [128, 128], F32).ap()
    a_s = nc.alloc_sbuf_tensor("a_s", [128, DIC * S], F32).ap()
    db_s = nc.alloc_sbuf_tensor("db_s", [128, DIC], F32).ap()
    dp_s = nc.alloc_sbuf_tensor("dp_s", [128, DIC], F32).ap()
    cb_s = nc.alloc_sbuf_tensor("cb_s", [128, DIC], F32).ap()
    cw_s = nc.alloc_sbuf_tensor("cw_s", [128, DIC * 4], F32).ap()
    wxh_s = nc.alloc_sbuf_tensor("wxh_s", [128, DIC * 33], F16).ap()
    wxl_s = nc.alloc_sbuf_tensor("wxl_s", [128, DIC * 33], F16).ap()
    hcar_s = nc.alloc_sbuf_tensor("hcar_s", [128, DIC * S], F32).ap()
    halo_s = nc.alloc_sbuf_tensor("halo_s", [128, DIC * 3], F32).ap()

    AF32 = nc.alloc_sbuf_tensor("AF32", [128, 30976], F32).ap()
    AF16 = nc.alloc_sbuf_tensor("AF16", [128, 24576], F16).ap()
    ABF = nc.alloc_sbuf_tensor("ABF", [128, 16384], BF16).ap()

    psum = [nc.alloc_psum_tensor(f"pp{i}", [128, 512], F32).ap() for i in range(8)]

    class Carver:
        def __init__(self, ap, total):
            self.ap, self.total, self.off = ap, total, 0

        def take(self, cols):
            assert self.off + cols <= self.total, (self.off, cols, self.total)
            r = self.ap[:, self.off:self.off + cols]
            self.off += cols
            return r

        def reset(self):
            self.off = 0

    c32 = Carver(AF32, 30976)
    c16 = Carver(AF16, 24576)
    cbf = Carver(ABF, 16384)

    from collections import defaultdict
    ops = {e: [] for e in ENGS}
    cnt = defaultdict(int)
    SEMNAMES = ["LA", "LB", "LG", "T", "V", "S", "G", "ST", "SG"]
    sems = {}

    def emit(eng, f):
        ops[eng].append(f)

    def dma(qeng, out_ap, in_ap, sem="LA", waits=(), transpose=False):
        cnt[sem] += 16
        v = cnt[sem]
        w = tuple((s2, val) for (s2, val) in waits if val > 0)

        def f(e, _o=out_ap, _i=in_ap, _s=sem, _w=w, _t=transpose):
            for (s2, val) in _w:
                e.wait_ge(sems[s2], val)
            kw = {"transpose": True} if _t else {}
            e.dma_start(out=_o, in_=_i, **kw).then_inc(sems[_s], 16)
        emit(qeng, f)
        return v

    def step(eng, sem, waits, builders):
        cnt[sem] += 1
        v = cnt[sem]
        w = tuple((s2, val) for (s2, val) in waits if val > 0)
        bs = tuple(builders)
        assert bs

        def f(e, _w=w, _b=bs, _s=sem):
            for (s2, val) in _w:
                e.wait_ge(sems[s2], val)
            for b in _b[:-1]:
                b(e)
            _b[-1](e).then_inc(sems[_s], 1)
        emit(eng, f)
        return v

    def waitonly(eng, waits):
        w = tuple((s2, val) for (s2, val) in waits if val > 0)

        def f(e, _w=w):
            for (s2, val) in _w:
                e.wait_ge(sems[s2], val)
        emit(eng, f)

    def barrier():
        vals = [(s, cnt[s]) for s in ("T", "V", "S", "G", "LA", "LB", "LG", "ST", "SG")]
        for eng in ENGS:
            waitonly(eng, vals)

    def csts(i):
        return cst_s[:, i:i + 1]

    bank_free = [("V", 0)] * 8

    def emit_qd(src, F, qa, qb, sc32, scbf, waits):
        B = F // 16
        w0 = sc32["w0"][:, :F]
        vv = sc32["vv"][:, :F]
        bm = sc32["bm"][:, :B]
        sc = sc32["sc"][:, :B]
        rs = sc32["rs"][:, :B]
        shb = scbf["shb"][:, :B]
        smb = scbf["smb"][:, :B]
        mk = [scbf[f"m{i}"][:, :F] for i in range(5)]

        srcg = src.rearrange("p (n b) -> p n b", b=16)
        w0g = w0.rearrange("p (n b) -> p n b", b=16)
        qag = qa.rearrange("p (n b) -> p n b", b=16)
        qbg = qb.rearrange("p (n b) -> p n b", b=16)

        v1 = step("vector", "V", waits, [
            lambda e: e.tensor_reduce(bm, srcg, AX.X, ALU.max, apply_absolute_value=True),
            lambda e: (e.drain(), e.tensor_scalar(sc, bm, 1.0 / 6.0, 1e-12, ALU.mult, ALU.max))[1],
            lambda e: (e.drain(), e.reciprocal(rs, sc))[1],
            lambda e: e.tensor_copy(shb, sc),
            lambda e: (e.drain(), e.tensor_tensor(smb, sc, shb, ALU.subtract))[1],
            lambda e: e.tensor_tensor(w0g, srcg,
                                      rs.unsqueeze(2).broadcast_to([128, B, 16]),
                                      ALU.mult),
        ])
        s1 = step("scalar", "S", [("V", v1)], [
            lambda e: e.activation(vv, w0, AF.Abs),
            lambda e: e.activation(mk[0], vv, AF.Sign, bias=csts(IC_M25)),
            lambda e: e.activation(mk[1], vv, AF.Sign, bias=csts(IC_M75)),
            lambda e: e.activation(mk[3], w0, AF.Sign),
        ])
        v2 = step("vector", "V", [("S", s1)], [
            lambda e: e.tensor_tensor(mk[0], mk[0], mk[1], ALU.add),
        ])
        s2 = step("scalar", "S", [("V", v2)], [
            lambda e: e.activation(mk[1], vv, AF.Sign, bias=csts(IC_M125)),
            lambda e: e.activation(mk[2], vv, AF.Sign, bias=csts(IC_M175)),
            lambda e: e.activation(mk[4], vv, AF.Sign, bias=csts(IC_M25X)),
        ])
        v3 = step("vector", "V", [("S", s2)], [
            lambda e: e.tensor_tensor(mk[1], mk[1], mk[2], ALU.add),
            lambda e: e.tensor_tensor(mk[0], mk[0], mk[1], ALU.add),
        ])
        s3 = step("scalar", "S", [("V", v3)], [
            lambda e: e.activation(mk[1], vv, AF.Sign, bias=csts(IC_M35X)),
            lambda e: e.activation(mk[2], vv, AF.Sign, bias=csts(IC_M5X)),
        ])
        v4 = step("vector", "V", [("S", s3)], [
            lambda e: e.tensor_tensor(mk[4], mk[4], mk[1], ALU.add),
            lambda e: e.tensor_scalar(mk[0], mk[0], 0.5, None, ALU.mult),
            lambda e: e.tensor_tensor(mk[0], mk[0], mk[4], ALU.add),
            lambda e: e.tensor_scalar(mk[0], mk[0], 0.5, 3.0, ALU.mult, ALU.add),
            lambda e: e.tensor_tensor(mk[0], mk[0], mk[2], ALU.add),
            lambda e: e.tensor_tensor(mk[0], mk[0], mk[3], ALU.mult),
            lambda e: e.tensor_tensor(qag, mk[0].rearrange("p (n b) -> p n b", b=16),
                                      shb.unsqueeze(2).broadcast_to([128, B, 16]),
                                      ALU.mult),
            lambda e: e.tensor_tensor(qbg, mk[0].rearrange("p (n b) -> p n b", b=16),
                                      smb.unsqueeze(2).broadcast_to([128, B, 16]),
                                      ALU.mult),
        ])
        return v4

    def qd_scratch(K):
        sc32 = {"w0": c32.take(K), "vv": c32.take(K),
                "bm": c32.take(K // 16), "sc": c32.take(K // 16),
                "rs": c32.take(K // 16)}
        scbf = {"shb": cbf.take(K // 16), "smb": cbf.take(K // 16)}
        for i in range(5):
            scbf[f"m{i}"] = cbf.take(K)
        return sc32, scbf

    c32.reset()
    wxf = c32.take(DIC * 33)
    dma("sync", cst_s, cst_d)
    dma("sync", idn_s, idn_d)
    dma("sync", a_s, alog_d)
    dma("sync", db_s, db_d)
    dma("sync", dp_s, dp_d)
    dma("sync", cb_s, cb_d)
    dma("sync", cw_s, cw_d)
    l1 = dma("sync", wxf, wx_d)
    sP0 = step("scalar", "S", [("LA", l1)], [
        lambda e: e.activation(a_s, a_s, AF.Exp),
    ])
    step("vector", "V", [("S", sP0)], [
        lambda e: e.tensor_scalar(a_s, a_s, -1.0, None, ALU.mult),
        lambda e: e.tensor_copy(wxh_s, wxf),
        lambda e: e.tensor_tensor(wxl_s, wxf, wxh_s, ALU.subtract),
        lambda e: e.memset(hcar_s, 0.0),
        lambda e: e.memset(halo_s, 0.0),
    ])
    barrier()

    def split_stream(src_d, dst_h, dst_l, nrows, K):
        c32.reset()
        c16.reset()
        wf = [c32.take(K), c32.take(K)]
        hh = [c16.take(K), c16.take(K)]
        ll = [c16.take(K), c16.take(K)]
        prev_v = [0, 0]
        prev_st = [0, 0]
        for i in range(nrows // 128):
            p = i % 2
            rows = slice(i * 128, (i + 1) * 128)
            lg = dma("gpsimd", hh[p], src_d[rows, :], sem="LG",
                     waits=[("ST", prev_st[p]), ("V", prev_v[p])])
            la = dma("sync", wf[p], src_d[rows, :], sem="LA",
                     waits=[("V", prev_v[p])])
            v = step("vector", "V", [("LG", lg), ("LA", la)], [
                lambda e, _l=ll[p], _w=wf[p], _h=hh[p]:
                    e.tensor_tensor(_l, _w, _h, ALU.subtract),
            ])
            dma("sync", dst_h[rows, :], hh[p], sem="ST", waits=[("LG", lg)])
            st2 = dma("sync", dst_l[rows, :], ll[p], sem="ST", waits=[("V", v)])
            prev_v[p] = v
            prev_st[p] = st2

    split_stream(win_d, winh_d, winl_d, DM, 2 * DI)
    split_stream(wout_d, wouth_d, woutl_d, DI, DM)
    barrier()

    def qd_stream(src_d, qa_d, qb_d, nrows, K):
        c32.reset()
        c16.reset()
        cbf.reset()
        sc32, scbf = qd_scratch(K)
        wf = [c32.take(K), c32.take(K)]
        qa = [c16.take(K), c16.take(K)]
        qb = [c16.take(K), c16.take(K)]
        pv = [cnt["V"], cnt["V"]]
        pst = [0, 0]
        for i in range(nrows // 128):
            p = i % 2
            rows = slice(i * 128, (i + 1) * 128)
            la = dma("sync", wf[p], src_d[rows, :], sem="LA", waits=[("V", pv[p])])
            v = emit_qd(wf[p], K, qa[p], qb[p], sc32, scbf,
                        waits=[("LA", la), ("SG", pst[p])])
            dma("gpsimd", qa_d[rows, :], qa[p], sem="SG", waits=[("V", v)])
            pst[p] = dma("gpsimd", qb_d[rows, :], qb[p], sem="SG", waits=[("V", v)])
            pv[p] = v

    qd_stream(wg_d, qwga_d, qwgb_d, NF, DM)
    qd_stream(wu_d, qwua_d, qwub_d, NF, DM)
    qd_stream(wd_d, qwda_d, qwdb_d, DM, NF)
    barrier()
    if stage == "p1":
        return finish(nc, ops, sems, cnt)

    c32.reset()
    c16.reset()
    cbf.reset()
    xn_t = [c32.take(DM) for _ in range(16)]
    mu = c32.take(1)
    ssum = c32.take(1)
    rvs = c32.take(1)
    rstd = c32.take(1)
    sqbuf = c32.take(DM)
    ln_done = []
    for i in range(16):
        rows = slice(i * 128, (i + 1) * 128)
        la = dma("sync", xn_t[i], x_d[rows, :], sem="LA")
        v1 = step("vector", "V", [("LA", la)], [
            lambda e, _x=xn_t[i]: e.reduce_sum(mu, _x, AX.X),
            lambda e: (e.drain(), e.tensor_scalar(mu, mu, 1.0 / DM, None, ALU.mult))[1],
            lambda e, _x=xn_t[i]: (e.drain(), e.tensor_scalar(_x, _x, mu, None, ALU.subtract))[1],
        ])
        s1 = step("scalar", "S", [("V", v1)], [
            lambda e, _x=xn_t[i]: e.activation(sqbuf, _x, AF.Square),
        ])
        v2 = step("vector", "V", [("S", s1)], [
            lambda e: e.reduce_sum(ssum, sqbuf, AX.X),
            lambda e: (e.drain(), e.tensor_scalar(rvs, ssum, 1.0 / DM, 1e-5, ALU.mult, ALU.add))[1],
            lambda e: (e.drain(), e.reciprocal(rvs, rvs))[1],
        ])
        s2 = step("scalar", "S", [("V", v2)], [
            lambda e: e.activation(rstd, rvs, AF.Sqrt),
        ])
        v3 = step("vector", "V", [("S", s2)], [
            lambda e, _x=xn_t[i]: (e.drain(), e.tensor_scalar(_x, _x, rstd, None, ALU.mult))[1],
        ])
        ln_done.append(v3)

    sc32, scbf = qd_scratch(DM)
    xh = c16.take(DM)
    xl = c16.take(DM)
    qa = c16.take(DM)
    qb = c16.take(DM)
    pst = 0
    for i in range(16):
        rows = slice(i * 128, (i + 1) * 128)
        v1 = step("vector", "V", [("V", ln_done[i]), ("ST", pst)], [
            lambda e, _x=xn_t[i]: e.tensor_copy(xh, _x),
            lambda e, _x=xn_t[i]: e.tensor_tensor(xl, _x, xh, ALU.subtract),
        ])
        dma("sync", xnh_d[rows, :], xh, sem="ST", waits=[("V", v1)])
        dma("sync", xnl_d[rows, :], xl, sem="ST", waits=[("V", v1)])
        v2 = emit_qd(xn_t[i], DM, qa, qb, sc32, scbf, waits=[("V", v1)])
        dma("sync", qxa_d[rows, :], qa, sem="ST", waits=[("V", v2)])
        pst = dma("sync", qxb_d[rows, :], qb, sem="ST", waits=[("V", v2)])
    barrier()
    if stage == "p2":
        return finish(nc, ops, sems, cnt)

    NOFFS = [0, 512, 1024, 1536, 2048, 2560]
    NSIZES = [512, 512, 512, 512, 512, 256]
    c32.reset()
    c16.reset()
    cbf.reset()
    h_t = [c32.take(NF) for _ in range(4)]
    sg = [c32.take(512) for _ in range(4)]
    p3a_sc32 = {"w0": c32.take(NF), "vv": c32.take(NF),
                "bm": c32.take(NF // 16), "sc": c32.take(NF // 16),
                "rs": c32.take(NF // 16)}
    qh_a = c32.take(NF // 2).bitcast(F16)
    qh_b = c32.take(NF // 2).bitcast(F16)
    p3a_scbf = {"shb": c32.take(NF // 32 + 8).bitcast(BF16),
                "smb": c32.take(NF // 32 + 8).bitcast(BF16)}
    for _mi in range(5):
        p3a_scbf[f"m{_mi}"] = c32.take(NF // 2).bitcast(BF16)
    qxTa = [c16.take(512) for _ in range(KC)]
    qxTb = [c16.take(512) for _ in range(KC)]
    qwt0 = {nm: [c16.take(512) for _ in range(KC)]
            for nm in ("ga", "gb", "ua", "ub")}
    abf16 = ABF.bitcast(F16)
    qwt1 = {nm: [abf16[:, (j * KC + k) * 512:(j * KC + k + 1) * 512]
                 for k in range(KC)]
            for j, nm in enumerate(("ga", "gb", "ua", "ub"))}
    qwts = [qwt0, qwt1]
    mm_done_par = [0, 0]
    h_evac = [0, 0, 0, 0]
    p3a_pst = [0]
    for tb in range(4):
        t0 = tb * 512
        lx = 0
        for k in range(KC):
            w = [("ST", cnt["ST"]), ("SG", cnt["SG"]), ("T", cnt["T"])] if k == 0 else []
            lx = dma("sync", qxTa[k], qxa_d[t0:t0 + 512, k * 128:(k + 1) * 128],
                     sem="LA", waits=w, transpose=True)
            lx = dma("sync", qxTb[k], qxb_d[t0:t0 + 512, k * 128:(k + 1) * 128],
                     sem="LA", transpose=True)
        for ni, (n0, nsz) in enumerate(zip(NOFFS, NSIZES)):
            par = ni % 2
            qwt = qwts[par]
            lw = 0
            for k in range(KC):
                kcols = slice(k * 128, (k + 1) * 128)
                nrows = slice(n0, n0 + nsz)
                w = [("T", mm_done_par[par])] if k == 0 else []
                lw = dma("sync", qwt["ga"][k][:, :nsz], qwga_d[nrows, kcols],
                         sem="LA", waits=w, transpose=True)
                lw = dma("sync", qwt["gb"][k][:, :nsz], qwgb_d[nrows, kcols],
                         sem="LA", transpose=True)
                lw = dma("sync", qwt["ua"][k][:, :nsz], qwua_d[nrows, kcols],
                         sem="LA", transpose=True)
                lw = dma("sync", qwt["ub"][k][:, :nsz], qwub_d[nrows, kcols],
                         sem="LA", transpose=True)
            for tt in range(4):
                bg, bu = (tt * 2) % 8, (tt * 2 + 1) % 8
                mm = []
                for k in range(KC):
                    sa = qxTa[k][:, tt * 128:(tt + 1) * 128]
                    sb = qxTb[k][:, tt * 128:(tt + 1) * 128]
                    fst, lst = (k == 0), (k == KC - 1)
                    mm.append(lambda e, _s=sa, _r=qwt["ga"][k][:, :nsz], _b=bg, _f=fst, _n=nsz:
                              e.matmul(psum[_b][:, :_n], _s, _r, start=_f, stop=False))
                    mm.append(lambda e, _s=sa, _r=qwt["ua"][k][:, :nsz], _b=bu, _f=fst, _n=nsz:
                              e.matmul(psum[_b][:, :_n], _s, _r, start=_f, stop=False))
                    mm.append(lambda e, _s=sa, _r=qwt["gb"][k][:, :nsz], _b=bg, _n=nsz:
                              e.matmul(psum[_b][:, :_n], _s, _r, start=False, stop=False))
                    mm.append(lambda e, _s=sa, _r=qwt["ub"][k][:, :nsz], _b=bu, _n=nsz:
                              e.matmul(psum[_b][:, :_n], _s, _r, start=False, stop=False))
                    mm.append(lambda e, _s=sb, _r=qwt["ga"][k][:, :nsz], _b=bg, _l=lst, _n=nsz:
                              e.matmul(psum[_b][:, :_n], _s, _r, start=False, stop=_l))
                    mm.append(lambda e, _s=sb, _r=qwt["ua"][k][:, :nsz], _b=bu, _l=lst, _n=nsz:
                              e.matmul(psum[_b][:, :_n], _s, _r, start=False, stop=_l))
                tmm = step("tensor", "T", [("LA", lw),
                                           bank_free[bg], bank_free[bu]], mm)
                mm_done_par[par] = tmm
                se = step("scalar", "S", [("T", tmm), ("V", h_evac[tt])], [
                    lambda e, _b=bg, _n=nsz, _g=sg[tt]: e.activation(_g[:, :_n], psum[_b][:, :_n], AF.Silu),
                ])
                ve = step("vector", "V", [("S", se)], [
                    lambda e, _b=bu, _h=h_t[tt][:, n0:n0 + nsz], _n=nsz, _g=sg[tt]:
                        e.tensor_tensor(_h, psum[_b][:, :_n], _g[:, :_n], ALU.mult),
                ])
                h_evac[tt] = ve
                bank_free[bg] = ("S", se)
                bank_free[bu] = ("V", ve)
        for tt in range(4):
            rows = slice(t0 + tt * 128, t0 + (tt + 1) * 128)
            v = emit_qd(h_t[tt], NF, qh_a, qh_b, p3a_sc32, p3a_scbf,
                        waits=[("V", h_evac[tt]), ("SG", p3a_pst[0])])
            dma("gpsimd", qha_d[rows, :], qh_a, sem="SG", waits=[("V", v)])
            p3a_pst[0] = dma("gpsimd", qhb_d[rows, :], qh_b, sem="SG", waits=[("V", v)])
            h_evac[tt] = v
    barrier()
    if stage == "p3a":
        return finish(nc, ops, sems, cnt)

    KQ = [list(range(0, 4)), list(range(4, 8)), list(range(8, 12)),
          list(range(12, 16)), list(range(16, 19)), list(range(19, 22))]
    for tb in range(2):
        c32.reset()
        t0 = tb * 1024
        facc = [c32.take(1024) for _ in range(8)]
        facc_v = [0] * 8
        p3c_mm_par = [0, 0]
        abf16b = ABF.bitcast(F16)
        for qi, kq in enumerate(KQ):
            if qi % 2 == 0:
                c16.reset()
                take2 = c16.take
            else:
                _off = [0]
                def take2(c, _o=_off):
                    r = abf16b[:, _o[0]:_o[0] + c]
                    _o[0] += c
                    return r
            qhTa = {k: take2(1024) for k in kq}
            qhTb = {k: take2(1024) for k in kq}
            wda = {k: take2(1024) for k in kq}
            wdb = {k: take2(1024) for k in kq}
            la = 0
            for j, k in enumerate(kq):
                kcols = slice(k * 128, (k + 1) * 128)
                w = [("T", p3c_mm_par[qi % 2]), ("SG", cnt["SG"])] if j == 0 else []
                la = dma("sync", qhTa[k], qha_d[t0:t0 + 1024, kcols],
                         sem="LA", waits=w, transpose=True)
                la = dma("sync", qhTb[k], qhb_d[t0:t0 + 1024, kcols],
                         sem="LA", transpose=True)
                la = dma("sync", wda[k], qwda_d[:, kcols], sem="LA",
                         transpose=True)
                la = dma("sync", wdb[k], qwdb_d[:, kcols], sem="LA",
                         transpose=True)
            for d in range(8):
                dcols = slice(d * 128, (d + 1) * 128)
                b0, b1 = (d * 2) % 8, (d * 2 + 1) % 8
                mm = []
                for j, k in enumerate(kq):
                    fst, lst = (j == 0), (j == len(kq) - 1)
                    for th in range(2):
                        tsl2 = slice(th * 512, (th + 1) * 512)
                        bb = b0 if th == 0 else b1
                        mm.append(lambda e, _w=wda[k][:, dcols], _r=qhTa[k][:, tsl2], _b=bb, _f=fst:
                                  e.matmul(psum[_b], _w, _r, start=_f, stop=False))
                        mm.append(lambda e, _w=wda[k][:, dcols], _r=qhTb[k][:, tsl2], _b=bb:
                                  e.matmul(psum[_b], _w, _r, start=False, stop=False))
                        mm.append(lambda e, _w=wdb[k][:, dcols], _r=qhTa[k][:, tsl2], _b=bb, _l=lst:
                                  e.matmul(psum[_b], _w, _r, start=False, stop=_l))
                tmm = step("tensor", "T", [("LA", la),
                                           bank_free[b0], bank_free[b1]], mm)
                p3c_mm_par[qi % 2] = tmm
                if qi == 0:
                    ve = step("vector", "V", [("T", tmm)], [
                        lambda e, _b=b0, _f=facc[d][:, 0:512]: e.tensor_copy(_f, psum[_b]),
                        lambda e, _b=b1, _f=facc[d][:, 512:1024]: e.tensor_copy(_f, psum[_b]),
                    ])
                else:
                    ve = step("vector", "V", [("T", tmm), ("V", facc_v[d])], [
                        lambda e, _b=b0, _f=facc[d][:, 0:512]:
                            e.tensor_tensor(_f, _f, psum[_b], ALU.add),
                        lambda e, _b=b1, _f=facc[d][:, 512:1024]:
                            e.tensor_tensor(_f, _f, psum[_b], ALU.add),
                    ])
                facc_v[d] = ve
                bank_free[b0] = ("V", ve)
                bank_free[b1] = ("V", ve)
        for d in range(8):
            dma("gpsimd", ffn_d[d * 128:(d + 1) * 128, t0:t0 + 1024], facc[d],
                sem="SG", waits=[("V", facc_v[d])])
        barrier()
    if stage == "p3c":
        return finish(nc, ops, sems, cnt)

    c32.reset()
    c16.reset()
    cbf.reset()
    xbr = c32.take(DIC * 515)
    xcy = c32.take(DIC * 512)
    mmr = c32.take(DIC * 512)
    ssm_s = c32.take(512)
    dtb = c32.take(512)
    sp1 = c32.take(512)
    sp2 = c32.take(512)
    dAb = [c32.take(512), c32.take(512)]
    ub = [c32.take(512), c32.take(512)]
    hbf2 = [c32.take(512), c32.take(512)]
    tmpb = sp2
    Bsb = [c32.take(512), c32.take(512)]
    Csb = [dtb, ssm_s]
    szb = sp1

    xnTh = [c16.take(512) for _ in range(KC)]
    xnTl = [c16.take(512) for _ in range(KC)]
    wstah = [c16.take(2048), c16.take(2048)]
    wstal = [c16.take(2048), c16.take(2048)]
    y2h = c16.take(DIC * 512)
    y2l = ABF.bitcast(F16)[:, 0:DIC * 512]
    xchb = y2h[:, 0:512]
    xclb = y2h[:, 512:1024]

    def xb_t(i):
        return xbr[:, i * 515:(i + 1) * 515]

    def dt_t(i):
        return xbr[:, i * 512:(i + 1) * 512]

    def xc_t(i):
        return xcy[:, i * 512:(i + 1) * 512]

    def m_t(i):
        return mmr[:, i * 512:(i + 1) * 512]

    sum_t = [mmr[:, i * 512:(i + 1) * 512] for i in range(8)]
    outtm = mmr[:, 8 * 512:8 * 512 + 1024]
    xtm = mmr[:, 8 * 512 + 1024:8 * 512 + 2048]

    def load_sta(dst, src2d, kchunks, w):
        return dma("gpsimd",
                   dst[:, :kchunks * 128].rearrange("p (k n) -> p k n", n=128),
                   src2d.rearrange("(k p) n -> p k n", p=128),
                   sem="LG", waits=w)

    for ch in range(NCH):
        tsl = slice(ch * 512, (ch + 1) * 512)
        lxn = 0
        for k in range(KC):
            kcols = slice(k * 128, (k + 1) * 128)
            w = [("ST", cnt["ST"]), ("SG", cnt["SG"]), ("T", cnt["T"])] if k == 0 else []
            lxn = dma("sync", xnTh[k], xnh_d[tsl, kcols], sem="LA",
                      waits=w, transpose=True)
            lxn = dma("sync", xnTl[k], xnl_d[tsl, kcols], sem="LA",
                      transpose=True)
        s_cv = []
        win_mm_par = [0, 0]
        for n in range(DIC):
            par = n % 2
            ncols = slice(n * 128, (n + 1) * 128)
            load_sta(wstah[par], winh_d[:, ncols], KC, [("T", win_mm_par[par])])
            lw = load_sta(wstal[par], winl_d[:, ncols], KC, [])
            bnk = n % 4
            mm = []
            for k in range(KC):
                fst, lst = (k == 0), (k == KC - 1)
                wh = wstah[par][:, k * 128:(k + 1) * 128]
                wl = wstal[par][:, k * 128:(k + 1) * 128]
                mm.append(lambda e, _w=wh, _r=xnTh[k], _b=bnk, _f=fst:
                          e.matmul(psum[_b], _w, _r, start=_f, stop=False))
                mm.append(lambda e, _w=wh, _r=xnTl[k], _b=bnk:
                          e.matmul(psum[_b], _w, _r, start=False, stop=False))
                mm.append(lambda e, _w=wl, _r=xnTh[k], _b=bnk, _l=lst:
                          e.matmul(psum[_b], _w, _r, start=False, stop=_l))
            tmm = step("tensor", "T", [("LG", lw), ("LA", lxn), bank_free[bnk]], mm)
            win_mm_par[par] = tmm
            s_ev = step("scalar", "S", [("T", tmm)], [
                lambda e, _b=bnk, _x=xb_t(n): e.activation(_x[:, 3:515], psum[_b], AF.Copy),
            ])
            bank_free[bnk] = ("S", s_ev)
            halo3 = halo_s[:, n * 3:(n + 1) * 3]
            if ch == 0:
                pre = [lambda e, _x=xb_t(n): e.memset(_x[:, 0:3], 0.0)]
            else:
                pre = [lambda e, _x=xb_t(n), _h=halo3: e.tensor_copy(_x[:, 0:3], _h)]
            vcv = step("vector", "V", [("S", s_ev)], pre + [
                lambda e, _x=xb_t(n), _c=cw_s[:, n * 4:n * 4 + 1]:
                    e.tensor_scalar(sp1, _x[:, 0:512], _c, None, ALU.mult),
                lambda e, _x=xb_t(n), _c=cw_s[:, n * 4 + 1:n * 4 + 2]:
                    e.scalar_tensor_tensor(sp2, _x[:, 1:513], _c, sp1, ALU.mult, ALU.add),
                lambda e, _x=xb_t(n), _c=cw_s[:, n * 4 + 2:n * 4 + 3]:
                    e.scalar_tensor_tensor(sp1, _x[:, 2:514], _c, sp2, ALU.mult, ALU.add),
                lambda e, _x=xb_t(n), _c=cw_s[:, n * 4 + 3:n * 4 + 4]:
                    e.scalar_tensor_tensor(sp2, _x[:, 3:515], _c, sp1, ALU.mult, ALU.add),
                lambda e, _x=xb_t(n), _h=halo3: e.tensor_copy(_h, _x[:, 512:515]),
            ])
            sc_ = step("scalar", "S", [("V", vcv)], [
                lambda e, _o=xc_t(n), _b=cb_s[:, n:n + 1]:
                    e.activation(_o, sp2, AF.Silu, bias=_b),
            ])
            s_cv.append(sc_)
            if "s_xb" in dbg:
                dma("sync", xb_dbg[n * 128:(n + 1) * 128, tsl], xb_t(n)[:, 3:515],
                    sem="ST", waits=[("V", vcv)])
            if "s_xc" in dbg:
                dma("sync", xc_dbg[n * 128:(n + 1) * 128, tsl], xc_t(n),
                    sem="ST", waits=[("S", sc_)])
        prev_wx_mm = 0
        for k in range(DIC):
            vch = step("vector", "V", [("S", s_cv[k]), ("T", prev_wx_mm)], [
                lambda e, _x=xc_t(k): e.tensor_copy(xchb, _x),
                lambda e, _x=xc_t(k): e.tensor_tensor(xclb, _x, xchb, ALU.subtract),
            ])
            fst, lst = (k == 0), (k == DIC - 1)
            wcols = slice(k * 33, (k + 1) * 33)
            mm = [
                lambda e, _w=wxh_s[:, wcols], _f=fst:
                    e.matmul(psum[4][0:33, :], _w, xchb, start=_f, stop=False),
                lambda e, _w=wxh_s[:, wcols]:
                    e.matmul(psum[4][0:33, :], _w, xclb, start=False, stop=False),
                lambda e, _w=wxl_s[:, wcols], _l=lst:
                    e.matmul(psum[4][0:33, :], _w, xchb, start=False, stop=_l),
            ]
            wts = [("V", vch)]
            if k == 0:
                wts.append(bank_free[4])
            tmm = step("tensor", "T", wts, mm)
            prev_wx_mm = tmm
        s_sx = step("scalar", "S", [("T", cnt["T"])], [
            lambda e: e.activation(ssm_s[0:33, :], psum[4][0:33, :], AF.Copy),
        ])
        bank_free[4] = ("S", s_sx)
        st_ssm = dma("sync", ssm_d[:, tsl], ssm_s[0:33, :], sem="ST",
                     waits=[("S", s_sx)])
        l_dtb = dma("sync", dtb, ssm_d[32:33, tsl].partition_broadcast(128),
                    sem="LA", waits=[("ST", st_ssm)])
        vdt_prev = 0
        for i in range(DIC):
            sdt = step("scalar", "S", [("LA", l_dtb), ("V", vdt_prev)], [
                lambda e, _b=db_s[:, i:i + 1]: e.activation(sp1, dtb, AF.Abs, bias=_b),
                lambda e: e.activation(sp2, sp1, AF.Exp, scale=csts(IC_NEG1)),
                lambda e: e.activation(sp1, sp2, AF.Ln, bias=csts(IC_1)),
            ])
            vdt_prev = step("vector", "V", [("S", sdt)], [
                lambda e, _b=db_s[:, i:i + 1]:
                    e.tensor_scalar(sp2, dtb, _b, 0.0, ALU.add, ALU.max),
                lambda e, _d=dt_t(i): e.tensor_tensor(_d, sp1, sp2, ALU.add),
            ])
        if "s_dt" in dbg:
            for i in range(DIC):
                dma("sync", dt_dbg[i * 128:(i + 1) * 128, tsl], dt_t(i),
                    sem="ST", waits=[("V", cnt["V"])])
        for i in range(DIC):
            step("vector", "V", [], [
                lambda e, _m=m_t(i), _d=dt_t(i), _x=xc_t(i):
                    e.tensor_tensor(_m, _d, _x, ALU.mult),
                lambda e, _x=xc_t(i), _p=dp_s[:, i:i + 1]:
                    e.tensor_scalar(_x, _x, _p, None, ALU.mult),
            ])
        VB, SB, GB = cnt["V"], cnt["S"], cnt["G"]
        bs_val = {}
        cs_val = {}
        for s in range(S):
            w = [] if s < 2 else [("V", VB + (s - 1) * DIC + 1)]
            wc = [("V", VB), ("ST", st_ssm)] if s < 2 else w
            bs_val[s] = dma("sync", Bsb[s % 2],
                            ssm_d[s:s + 1, tsl].partition_broadcast(128),
                            sem="LA", waits=w)
            cs_val[s] = dma("sync", Csb[s % 2],
                            ssm_d[S + s:S + s + 1, tsl].partition_broadcast(128),
                            sem="LA", waits=wc)
        prev_col = None
        for s in range(S):
            for ci in range(DIC):
                i = s * DIC + ci
                col = ci * S + s
                step("scalar", "S", [("V", max(VB, VB + i - 1))], [
                    lambda e, _o=dAb[i % 2], _d=dt_t(ci), _a=a_s[:, col:col + 1]:
                        e.activation(_o, _d, AF.Exp, scale=_a),
                ])
                gbl = [
                    lambda e, _o=ub[i % 2], _m=m_t(ci), _b=Bsb[s % 2]:
                        e.tensor_tensor(_o, _m, _b, ALU.mult),
                ]
                if i >= 2 and (i - 2) % 2 == 0:
                    s2 = (i - 2) // DIC
                    gbl.append(lambda e, _c=Csb[s2 % 2], _o=hbf2[(i - 2) % 2]:
                               e.tensor_tensor(sp1, _o, _c, ALU.mult))
                step("gpsimd", "G",
                     [("V", max(VB, VB + i - 1)), ("LA", bs_val[s])], gbl)
                bl = [
                    lambda e, _a=dAb[i % 2], _u=ub[i % 2], _o=hbf2[i % 2],
                           _h=hcar_s[:, col:col + 1]:
                        e.tensor_tensor_scan(_o, _a, _u, _h, ALU.mult, ALU.add),
                ]
                if prev_col is not None:
                    bl.append(lambda e, _h=hcar_s[:, prev_col:prev_col + 1],
                              _s=hbf2[(i - 1) % 2]:
                              e.tensor_copy(_h, _s[:, 511:512]))
                if i % 2 == 1:
                    bl += [
                        lambda e, _c=Csb[s % 2], _o=hbf2[i % 2]:
                            e.tensor_tensor(tmpb, _o, _c, ALU.mult),
                        lambda e, _y=xc_t(ci):
                            e.tensor_tensor(_y, _y, tmpb, ALU.add),
                    ]
                if i >= 2 and (i - 2) % 2 == 0:
                    ci2 = (i - 2) % DIC
                    bl.append(lambda e, _y=xc_t(ci2):
                              e.tensor_tensor(_y, _y, sp1, ALU.add))
                step("vector", "V",
                     [("S", SB + i + 1), ("G", GB + i + 1), ("LA", cs_val[s])], bl)
                prev_col = col
        NIT = S * DIC
        step("gpsimd", "G", [("V", VB + NIT)], [
            lambda e, _c=Csb[(S - 1) % 2], _o=hbf2[(NIT - 2) % 2]:
                e.tensor_tensor(sp1, _o, _c, ALU.mult),
        ])
        step("vector", "V", [("G", cnt["G"])], [
            lambda e, _h=hcar_s[:, prev_col:prev_col + 1],
                   _s=hbf2[(NIT - 1) % 2]:
                (e.drain(), e.tensor_copy(_h, _s[:, 511:512]))[1],
            lambda e, _y=xc_t(DIC - 2):
                e.tensor_tensor(_y, _y, sp1, ALU.add),
        ])
        if "s_y" in dbg:
            for i in range(DIC):
                dma("sync", y_dbg[i * 128:(i + 1) * 128, tsl], xc_t(i),
                    sem="ST", waits=[("V", cnt["V"])])
        v_y2p = cnt["V"]
        z_mm_par = [cnt["T"], cnt["T"]]
        for i in range(DIC):
            par = i % 2
            zcols = slice(DI + i * 128, DI + (i + 1) * 128)
            load_sta(wstah[par], winh_d[:, zcols], KC, [("T", z_mm_par[par])])
            lw = load_sta(wstal[par], winl_d[:, zcols], KC, [])
            bnk = i % 4
            mm = []
            for k in range(KC):
                fst, lst = (k == 0), (k == KC - 1)
                wh = wstah[par][:, k * 128:(k + 1) * 128]
                wl = wstal[par][:, k * 128:(k + 1) * 128]
                mm.append(lambda e, _w=wh, _r=xnTh[k], _b=bnk, _f=fst:
                          e.matmul(psum[_b], _w, _r, start=_f, stop=False))
                mm.append(lambda e, _w=wh, _r=xnTl[k], _b=bnk:
                          e.matmul(psum[_b], _w, _r, start=False, stop=False))
                mm.append(lambda e, _w=wl, _r=xnTh[k], _b=bnk, _l=lst:
                          e.matmul(psum[_b], _w, _r, start=False, stop=_l))
            tmm = step("tensor", "T", [("LG", lw), bank_free[bnk]], mm)
            z_mm_par[par] = tmm
            s_z = step("scalar", "S", [("T", tmm), ("V", v_y2p)], [
                lambda e, _b=bnk: e.activation(szb, psum[_b], AF.Silu),
            ])
            bank_free[bnk] = ("S", s_z)
            v_y2p = step("vector", "V", [("S", s_z)], [
                lambda e, _y=xc_t(i): e.tensor_tensor(_y, _y, szb, ALU.mult),
                lambda e, _y=xc_t(i), _o=y2h[:, i * 512:(i + 1) * 512]:
                    e.tensor_copy(_o, _y),
                lambda e, _y=xc_t(i), _h=y2h[:, i * 512:(i + 1) * 512],
                       _o=y2l[:, i * 512:(i + 1) * 512]:
                    e.tensor_tensor(_o, _y, _h, ALU.subtract),
            ])
        v_wo = [0] * 8
        wo_mm_par = [cnt["T"], cnt["T"]]
        for no in range(8):
            par = no % 2
            ncols = slice(no * 128, (no + 1) * 128)
            load_sta(wstah[par], wouth_d[:, ncols], DIC, [("T", wo_mm_par[par])])
            lw = load_sta(wstal[par], woutl_d[:, ncols], DIC, [])
            fl = dma("gpsimd", dAb[no % 2], ffn_d[ncols, tsl], sem="LG",
                     waits=[("V", v_wo[no - 2] if no >= 2 else cnt["V"])])
            bnk = 4 + no % 4
            mm = []
            for k in range(DIC):
                fst, lst = (k == 0), (k == DIC - 1)
                wh = wstah[par][:, k * 128:(k + 1) * 128]
                wl = wstal[par][:, k * 128:(k + 1) * 128]
                y2hs = y2h[:, k * 512:(k + 1) * 512]
                y2ls = y2l[:, k * 512:(k + 1) * 512]
                mm.append(lambda e, _w=wh, _r=y2hs, _b=bnk, _f=fst:
                          e.matmul(psum[_b], _w, _r, start=_f, stop=False))
                mm.append(lambda e, _w=wh, _r=y2ls, _b=bnk:
                          e.matmul(psum[_b], _w, _r, start=False, stop=False))
                mm.append(lambda e, _w=wl, _r=y2hs, _b=bnk, _l=lst:
                          e.matmul(psum[_b], _w, _r, start=False, stop=_l))
            tmm = step("tensor", "T", [("LG", max(lw, fl)), ("V", v_y2p), bank_free[bnk]], mm)
            wo_mm_par[par] = tmm
            v_wo[no] = step("vector", "V", [("T", tmm), ("LG", fl)], [
                lambda e, _b=bnk, _s=sum_t[no], _f=dAb[no % 2]:
                    e.tensor_tensor(_s, psum[_b], _f, ALU.add),
            ])
            bank_free[bnk] = ("V", v_wo[no])
        st_out = 0
        v_as = 0
        for tsub in range(4):
            rows = slice(ch * 512 + tsub * 128, ch * 512 + (tsub + 1) * 128)
            lxr = dma("gpsimd", xtm, x_d[rows, :], sem="LG",
                      waits=[("V", v_as)])
            for no in range(8):
                bnk = no % 2
                ttr = step("tensor", "T",
                           [bank_free[bnk], ("V", v_wo[no])], [
                    lambda e, _s=sum_t[no][:, tsub * 128:(tsub + 1) * 128], _b=bnk:
                        e.transpose(psum[_b][:, 0:128], _s, idn_s),
                ])
                v_as = step("vector", "V",
                            [("T", ttr), ("LG", lxr), ("SG", st_out)], [
                    lambda e, _b=bnk, _o=outtm[:, no * 128:(no + 1) * 128],
                           _x=xtm[:, no * 128:(no + 1) * 128]:
                        e.tensor_tensor(_o, psum[_b][:, 0:128], _x, ALU.add),
                ])
                bank_free[bnk] = ("V", v_as)
            st_out = dma("gpsimd", out_d[rows, :], outtm, sem="SG",
                         waits=[("V", v_as)])
        barrier()

    return finish(nc, ops, sems, cnt)


def finish(nc, ops, sems, cnt):
    import contextlib

    final = [(s, cnt[s]) for s in ("ST", "SG", "LA", "LB", "LG") if cnt[s] > 0]

    def tail(e, _f=tuple(final)):
        for s2, val in _f:
            e.wait_ge(sems[s2], val)
    ops["sync"].append(tail)

    with contextlib.ExitStack() as st:
        for n in ["LA", "LB", "LG", "T", "V", "S", "G", "ST", "SG"]:
            sems[n] = st.enter_context(nc.semaphore(n))
        block = st.enter_context(nc.Block())
        for eng in ENGS:
            fns = tuple(ops[eng])

            def body(e, _fns=fns):
                for f in _fns:
                    f(e)
            getattr(block, eng)(body)
    return nc


def _prep_maps(inputs):
    f32 = np.float32
    x = np.ascontiguousarray(inputs["L0_attn_x"], dtype=f32)
    win = np.ascontiguousarray(inputs["L0_attn_W_in"], dtype=f32)
    wout = np.ascontiguousarray(inputs["L0_attn_W_out"], dtype=f32)
    wg = np.ascontiguousarray(inputs["L0_ffn_w_gate"], dtype=f32)
    wu = np.ascontiguousarray(inputs["L0_ffn_w_up"], dtype=f32)
    wd = np.ascontiguousarray(inputs["L0_ffn_w_down"], dtype=f32)
    alog = np.ascontiguousarray(
        np.asarray(inputs["L0_attn_A_log"], dtype=f32)
        .reshape(DIC, 128, S).transpose(1, 0, 2).reshape(128, DIC * S))
    db = np.ascontiguousarray(
        np.asarray(inputs["L0_attn_dt_bias"], dtype=f32).reshape(DIC, 128).T)
    dp = np.ascontiguousarray(
        np.asarray(inputs["L0_attn_D_param"], dtype=f32).reshape(DIC, 128).T)
    cb = np.ascontiguousarray(
        np.asarray(inputs["L0_attn_conv_b"], dtype=f32).reshape(DIC, 128).T)
    cw = np.ascontiguousarray(
        np.asarray(inputs["L0_attn_conv_w"], dtype=f32)[:, 0, :]
        .reshape(DIC, 128, 4).transpose(1, 0, 2).reshape(128, DIC * 4))
    wx = np.ascontiguousarray(
        np.asarray(inputs["L0_attn_W_x"], dtype=f32)
        .reshape(DIC, 128, 33).transpose(1, 0, 2).reshape(128, DIC * 33))
    cst = np.tile(np.asarray(CST, dtype=f32)[None, :], (128, 1))
    idn = np.eye(128, dtype=f32)
    shared = dict(w_in=win, w_out=wout, w_gate=wg, w_up=wu, w_down=wd,
                  a_log=alog, dt_bias=db, d_param=dp, conv_b=cb, conv_w=cw,
                  w_x=wx, cst=cst, idn=idn)
    maps = []
    for i in range(8):
        m = dict(shared)
        m["x"] = np.ascontiguousarray(x[i])
        maps.append(m)
    return maps


def kernel(**inputs):
    nc = build()
    maps = _prep_maps(inputs)
    res = run_bass_kernel_spmd(nc, maps, core_ids=list(range(8)))
    out = np.stack([np.asarray(r["out"]) for r in res.results], axis=0)
    return out.astype(np.float32)


# version 12
# speedup vs baseline: 1.0631x; 1.0631x over previous
import numpy as np
import concourse.bass as bass
import concourse.mybir as mybir
from concourse.bass_utils import run_bass_kernel_spmd

F32 = mybir.dt.float32
F16 = mybir.dt.float16
BF16 = mybir.dt.bfloat16
AF = mybir.ActivationFunctionType
ALU = mybir.AluOpType
AX = mybir.AxisListType

T = 2048
DM = 1024
DI = 2048
S = 16
NF = 2816
TC = 512
NCH = T // TC
KC = DM // 128
DIC = DI // 128
NFC = NF // 128

CST = [-1.0, 1.0, -0.25, -0.75, -1.25, -1.75]
IC_NEG1, IC_1, IC_M25, IC_M75, IC_M125, IC_M175 = range(6)

ENGS = ("sync", "tensor", "vector", "scalar", "gpsimd")


def build(stage="full", dbg=()):
    nc = bass.Bass("TRN2", target_bir_lowering=False)

    def dram(name, shape, dtype):
        kind = "ExternalOutput" if name in dbg else "Internal"
        return nc.dram_tensor(name, shape, dtype, kind=kind).ap()

    x_d = nc.dram_tensor("x", [T, DM], F32, kind="ExternalInput").ap()
    win_d = nc.dram_tensor("w_in", [DM, 2 * DI], F32, kind="ExternalInput").ap()
    wout_d = nc.dram_tensor("w_out", [DI, DM], F32, kind="ExternalInput").ap()
    wg_d = nc.dram_tensor("w_gate", [NF, DM], F32, kind="ExternalInput").ap()
    wu_d = nc.dram_tensor("w_up", [NF, DM], F32, kind="ExternalInput").ap()
    wd_d = nc.dram_tensor("w_down", [DM, NF], F32, kind="ExternalInput").ap()
    alog_d = nc.dram_tensor("a_log", [128, DIC * S], F32, kind="ExternalInput").ap()
    db_d = nc.dram_tensor("dt_bias", [128, DIC], F32, kind="ExternalInput").ap()
    dp_d = nc.dram_tensor("d_param", [128, DIC], F32, kind="ExternalInput").ap()
    cb_d = nc.dram_tensor("conv_b", [128, DIC], F32, kind="ExternalInput").ap()
    cw_d = nc.dram_tensor("conv_w", [128, DIC * 4], F32, kind="ExternalInput").ap()
    wx_d = nc.dram_tensor("w_x", [128, DIC * 33], F32, kind="ExternalInput").ap()
    cst_d = nc.dram_tensor("cst", [128, len(CST)], F32, kind="ExternalInput").ap()
    idn_d = nc.dram_tensor("idn", [128, 128], F32, kind="ExternalInput").ap()
    out_d = nc.dram_tensor("out", [T, DM], F32, kind="ExternalOutput").ap()

    winh_d = dram("s_winh", [DM, 2 * DI], F16)
    winl_d = dram("s_winl", [DM, 2 * DI], F16)
    wouth_d = dram("s_wouth", [DI, DM], F16)
    woutl_d = dram("s_woutl", [DI, DM], F16)
    qwga_d = dram("s_qwga", [NF, DM], F16)
    qwgb_d = dram("s_qwgb", [NF, DM], F16)
    qwua_d = dram("s_qwua", [NF, DM], F16)
    qwub_d = dram("s_qwub", [NF, DM], F16)
    qwda_d = dram("s_qwda", [DM, NF], F16)
    qwdb_d = dram("s_qwdb", [DM, NF], F16)
    xnh_d = dram("s_xnh", [T, DM], F16)
    xnl_d = dram("s_xnl", [T, DM], F16)
    qxa_d = dram("s_qxa", [T, DM], F16)
    qxb_d = dram("s_qxb", [T, DM], F16)
    ssm_d = dram("s_ssm", [33, T], F32)
    qha_d = dram("s_qha", [T, NF], F16)
    qhb_d = dram("s_qhb", [T, NF], F16)
    ffn_d = dram("s_ffn", [DM, T], F32)
    xb_dbg = dram("s_xb", [DI, T], F32)
    xc_dbg = dram("s_xc", [DI, T], F32)
    dt_dbg = dram("s_dt", [DI, T], F32)
    y_dbg = dram("s_y", [DI, T], F32)

    cst_s = nc.alloc_sbuf_tensor("cst_s", [128, len(CST)], F32).ap()
    idn_s = nc.alloc_sbuf_tensor("idn_s", [128, 128], F32).ap()
    a_s = nc.alloc_sbuf_tensor("a_s", [128, DIC * S], F32).ap()
    db_s = nc.alloc_sbuf_tensor("db_s", [128, DIC], F32).ap()
    dp_s = nc.alloc_sbuf_tensor("dp_s", [128, DIC], F32).ap()
    cb_s = nc.alloc_sbuf_tensor("cb_s", [128, DIC], F32).ap()
    cw_s = nc.alloc_sbuf_tensor("cw_s", [128, DIC * 4], F32).ap()
    wxh_s = nc.alloc_sbuf_tensor("wxh_s", [128, DIC * 33], F16).ap()
    wxl_s = nc.alloc_sbuf_tensor("wxl_s", [128, DIC * 33], F16).ap()
    hcar_s = nc.alloc_sbuf_tensor("hcar_s", [128, DIC * S], F32).ap()
    halo_s = nc.alloc_sbuf_tensor("halo_s", [128, DIC * 3], F32).ap()

    AF32 = nc.alloc_sbuf_tensor("AF32", [128, 30976], F32).ap()
    AF16 = nc.alloc_sbuf_tensor("AF16", [128, 24576], F16).ap()
    ABF = nc.alloc_sbuf_tensor("ABF", [128, 16384], BF16).ap()

    psum = [nc.alloc_psum_tensor(f"pp{i}", [128, 512], F32).ap() for i in range(8)]

    class Carver:
        def __init__(self, ap, total):
            self.ap, self.total, self.off = ap, total, 0

        def take(self, cols):
            assert self.off + cols <= self.total, (self.off, cols, self.total)
            r = self.ap[:, self.off:self.off + cols]
            self.off += cols
            return r

        def reset(self):
            self.off = 0

    c32 = Carver(AF32, 30976)
    c16 = Carver(AF16, 24576)
    cbf = Carver(ABF, 16384)

    from collections import defaultdict
    ops = {e: [] for e in ENGS}
    cnt = defaultdict(int)
    SEMNAMES = ["LA", "LB", "LG", "T", "V", "S", "G", "ST", "SG"]
    sems = {}

    def emit(eng, f):
        ops[eng].append(f)

    def dma(qeng, out_ap, in_ap, sem="LA", waits=(), transpose=False):
        cnt[sem] += 16
        v = cnt[sem]
        w = tuple((s2, val) for (s2, val) in waits if val > 0)

        def f(e, _o=out_ap, _i=in_ap, _s=sem, _w=w, _t=transpose):
            for (s2, val) in _w:
                e.wait_ge(sems[s2], val)
            kw = {"transpose": True} if _t else {}
            e.dma_start(out=_o, in_=_i, **kw).then_inc(sems[_s], 16)
        emit(qeng, f)
        return v

    def step(eng, sem, waits, builders):
        cnt[sem] += 1
        v = cnt[sem]
        w = tuple((s2, val) for (s2, val) in waits if val > 0)
        bs = tuple(builders)
        assert bs

        def f(e, _w=w, _b=bs, _s=sem):
            for (s2, val) in _w:
                e.wait_ge(sems[s2], val)
            for b in _b[:-1]:
                b(e)
            _b[-1](e).then_inc(sems[_s], 1)
        emit(eng, f)
        return v

    def waitonly(eng, waits):
        w = tuple((s2, val) for (s2, val) in waits if val > 0)

        def f(e, _w=w):
            for (s2, val) in _w:
                e.wait_ge(sems[s2], val)
        emit(eng, f)

    def barrier():
        vals = [(s, cnt[s]) for s in ("T", "V", "S", "G", "LA", "LB", "LG", "ST", "SG")]
        for eng in ENGS:
            waitonly(eng, vals)

    def csts(i):
        return cst_s[:, i:i + 1]

    bank_free = [("V", 0)] * 8

    def emit_qd(src, F, qa, qb, sc32, scbf, waits):
        B = F // 16
        w0 = sc32["w0"][:, :F]
        vv = sc32["vv"][:, :F]
        bm = sc32["bm"][:, :B]
        sc = sc32["sc"][:, :B]
        rs = sc32["rs"][:, :B]
        shb = scbf["shb"][:, :B]
        smb = scbf["smb"][:, :B]
        mk = [scbf[f"m{i}"][:, :F] for i in range(4)]

        srcg = src.rearrange("p (n b) -> p n b", b=16)
        w0g = w0.rearrange("p (n b) -> p n b", b=16)
        qag = qa.rearrange("p (n b) -> p n b", b=16)
        qbg = qb.rearrange("p (n b) -> p n b", b=16)

        v1 = step("vector", "V", waits, [
            lambda e: e.tensor_reduce(bm, srcg, AX.X, ALU.max, apply_absolute_value=True),
            lambda e: (e.drain(), e.tensor_scalar(sc, bm, 1.0 / 6.0, 1e-12, ALU.mult, ALU.max))[1],
            lambda e: (e.drain(), e.reciprocal(rs, sc))[1],
            lambda e: e.tensor_copy(shb, sc),
            lambda e: (e.drain(), e.tensor_tensor(smb, sc, shb, ALU.subtract))[1],
            lambda e: e.tensor_tensor(w0g, srcg,
                                      rs.unsqueeze(2).broadcast_to([128, B, 16]),
                                      ALU.mult),
        ])
        s1 = step("scalar", "S", [("V", v1)], [
            lambda e: e.activation(vv, w0, AF.Abs),
            lambda e: e.activation(mk[0], vv, AF.Sign, bias=csts(IC_M25)),
            lambda e: e.activation(mk[1], vv, AF.Sign, bias=csts(IC_M75)),
            lambda e: e.activation(mk[2], vv, AF.Sign, bias=csts(IC_M125)),
        ])
        v2 = step("vector", "V", [("S", s1)], [
            lambda e: e.tensor_tensor(mk[0], mk[0], mk[1], ALU.add),
        ])
        s2 = step("scalar", "S", [("V", v2)], [
            lambda e: e.activation(mk[1], vv, AF.Sign, bias=csts(IC_M175)),
            lambda e: e.activation(mk[3], w0, AF.Sign),
        ])
        v3 = step("vector", "V", [("S", s2)], [
            lambda e: e.tensor_tensor(mk[1], mk[1], mk[2], ALU.add),
            lambda e: e.tensor_tensor(mk[0], mk[0], mk[1], ALU.add),
            lambda e: e.tensor_scalar(mk[1], vv, 2.5, None, ALU.is_gt),
            lambda e: e.tensor_scalar(mk[2], vv, 3.5, None, ALU.is_gt),
            lambda e: e.tensor_tensor(mk[1], mk[1], mk[2], ALU.add),
            lambda e: e.tensor_scalar(mk[2], vv, 5.0, None, ALU.is_gt),
            lambda e: e.tensor_scalar(mk[2], mk[2], 2.0, None, ALU.mult),
            lambda e: e.tensor_tensor(mk[1], mk[1], mk[2], ALU.add),
            lambda e: e.tensor_scalar(mk[0], mk[0], 0.25, 1.0, ALU.mult, ALU.add),
            lambda e: e.tensor_tensor(mk[0], mk[0], mk[1], ALU.add),
            lambda e: e.tensor_tensor(mk[0], mk[0], mk[3], ALU.mult),
            lambda e: e.tensor_tensor(qag, mk[0].rearrange("p (n b) -> p n b", b=16),
                                      shb.unsqueeze(2).broadcast_to([128, B, 16]),
                                      ALU.mult),
            lambda e: e.tensor_tensor(qbg, mk[0].rearrange("p (n b) -> p n b", b=16),
                                      smb.unsqueeze(2).broadcast_to([128, B, 16]),
                                      ALU.mult),
        ])
        return v3

    def qd_scratch(K):
        sc32 = {"w0": c32.take(K), "vv": c32.take(K),
                "bm": c32.take(K // 16), "sc": c32.take(K // 16),
                "rs": c32.take(K // 16)}
        scbf = {"shb": cbf.take(K // 16), "smb": cbf.take(K // 16)}
        for i in range(4):
            scbf[f"m{i}"] = cbf.take(K)
        return sc32, scbf

    c32.reset()
    wxf = c32.take(DIC * 33)
    dma("sync", cst_s, cst_d)
    dma("sync", idn_s, idn_d)
    dma("sync", a_s, alog_d)
    dma("sync", db_s, db_d)
    dma("sync", dp_s, dp_d)
    dma("sync", cb_s, cb_d)
    dma("sync", cw_s, cw_d)
    l1 = dma("sync", wxf, wx_d)
    sP0 = step("scalar", "S", [("LA", l1)], [
        lambda e: e.activation(a_s, a_s, AF.Exp),
    ])
    step("vector", "V", [("S", sP0)], [
        lambda e: e.tensor_scalar(a_s, a_s, -1.0, None, ALU.mult),
        lambda e: e.tensor_copy(wxh_s, wxf),
        lambda e: e.tensor_tensor(wxl_s, wxf, wxh_s, ALU.subtract),
        lambda e: e.memset(hcar_s, 0.0),
        lambda e: e.memset(halo_s, 0.0),
    ])
    barrier()

    def split_stream(src_d, dst_h, dst_l, nrows, K):
        c32.reset()
        c16.reset()
        wf = [c32.take(K), c32.take(K)]
        hh = [c16.take(K), c16.take(K)]
        ll = [c16.take(K), c16.take(K)]
        prev_v = [0, 0]
        prev_st = [0, 0]
        for i in range(nrows // 128):
            p = i % 2
            rows = slice(i * 128, (i + 1) * 128)
            lg = dma("gpsimd", hh[p], src_d[rows, :], sem="LG",
                     waits=[("ST", prev_st[p]), ("V", prev_v[p])])
            la = dma("sync", wf[p], src_d[rows, :], sem="LA",
                     waits=[("V", prev_v[p])])
            v = step("vector", "V", [("LG", lg), ("LA", la)], [
                lambda e, _l=ll[p], _w=wf[p], _h=hh[p]:
                    e.tensor_tensor(_l, _w, _h, ALU.subtract),
            ])
            dma("sync", dst_h[rows, :], hh[p], sem="ST", waits=[("LG", lg)])
            st2 = dma("sync", dst_l[rows, :], ll[p], sem="ST", waits=[("V", v)])
            prev_v[p] = v
            prev_st[p] = st2

    split_stream(win_d, winh_d, winl_d, DM, 2 * DI)
    split_stream(wout_d, wouth_d, woutl_d, DI, DM)
    barrier()

    def qd_stream(src_d, qa_d, qb_d, nrows, K):
        c32.reset()
        c16.reset()
        cbf.reset()
        sc32, scbf = qd_scratch(K)
        wf = [c32.take(K), c32.take(K)]
        qa = [c16.take(K), c16.take(K)]
        qb = [c16.take(K), c16.take(K)]
        pv = [cnt["V"], cnt["V"]]
        pst = [0, 0]
        for i in range(nrows // 128):
            p = i % 2
            rows = slice(i * 128, (i + 1) * 128)
            la = dma("sync", wf[p], src_d[rows, :], sem="LA", waits=[("V", pv[p])])
            v = emit_qd(wf[p], K, qa[p], qb[p], sc32, scbf,
                        waits=[("LA", la), ("SG", pst[p])])
            dma("gpsimd", qa_d[rows, :], qa[p], sem="SG", waits=[("V", v)])
            pst[p] = dma("gpsimd", qb_d[rows, :], qb[p], sem="SG", waits=[("V", v)])
            pv[p] = v

    qd_stream(wg_d, qwga_d, qwgb_d, NF, DM)
    qd_stream(wu_d, qwua_d, qwub_d, NF, DM)
    barrier()
    if stage == "p1":
        return finish(nc, ops, sems, cnt)

    c32.reset()
    c16.reset()
    cbf.reset()
    xn_t = [c32.take(DM) for _ in range(16)]
    mu = c32.take(1)
    ssum = c32.take(1)
    rvs = c32.take(1)
    rstd = c32.take(1)
    sqbuf = c32.take(DM)
    ln_done = []
    for i in range(16):
        rows = slice(i * 128, (i + 1) * 128)
        la = dma("sync", xn_t[i], x_d[rows, :], sem="LA")
        v1 = step("vector", "V", [("LA", la)], [
            lambda e, _x=xn_t[i]: e.reduce_sum(mu, _x, AX.X),
            lambda e: (e.drain(), e.tensor_scalar(mu, mu, 1.0 / DM, None, ALU.mult))[1],
            lambda e, _x=xn_t[i]: (e.drain(), e.tensor_scalar(_x, _x, mu, None, ALU.subtract))[1],
        ])
        s1 = step("scalar", "S", [("V", v1)], [
            lambda e, _x=xn_t[i]: e.activation(sqbuf, _x, AF.Square),
        ])
        v2 = step("vector", "V", [("S", s1)], [
            lambda e: e.reduce_sum(ssum, sqbuf, AX.X),
            lambda e: (e.drain(), e.tensor_scalar(rvs, ssum, 1.0 / DM, 1e-5, ALU.mult, ALU.add))[1],
            lambda e: (e.drain(), e.reciprocal(rvs, rvs))[1],
        ])
        s2 = step("scalar", "S", [("V", v2)], [
            lambda e: e.activation(rstd, rvs, AF.Sqrt),
        ])
        v3 = step("vector", "V", [("S", s2)], [
            lambda e, _x=xn_t[i]: (e.drain(), e.tensor_scalar(_x, _x, rstd, None, ALU.mult))[1],
        ])
        ln_done.append(v3)

    sc32, scbf = qd_scratch(DM)
    xh = c16.take(DM)
    xl = c16.take(DM)
    qa = c16.take(DM)
    qb = c16.take(DM)
    pst = 0
    for i in range(16):
        rows = slice(i * 128, (i + 1) * 128)
        v1 = step("vector", "V", [("V", ln_done[i]), ("ST", pst)], [
            lambda e, _x=xn_t[i]: e.tensor_copy(xh, _x),
            lambda e, _x=xn_t[i]: e.tensor_tensor(xl, _x, xh, ALU.subtract),
        ])
        dma("sync", xnh_d[rows, :], xh, sem="ST", waits=[("V", v1)])
        dma("sync", xnl_d[rows, :], xl, sem="ST", waits=[("V", v1)])
        v2 = emit_qd(xn_t[i], DM, qa, qb, sc32, scbf, waits=[("V", v1)])
        dma("sync", qxa_d[rows, :], qa, sem="ST", waits=[("V", v2)])
        pst = dma("sync", qxb_d[rows, :], qb, sem="ST", waits=[("V", v2)])
    barrier()
    if stage == "p2":
        return finish(nc, ops, sems, cnt)

    NOFFS = [0, 512, 1024, 1536, 2048, 2560]
    NSIZES = [512, 512, 512, 512, 512, 256]
    c32.reset()
    c16.reset()
    cbf.reset()
    h_t = [c32.take(NF) for _ in range(4)]
    sg = [c32.take(512) for _ in range(4)]
    p3a_sc32 = {"w0": c32.take(NF), "vv": c32.take(NF),
                "bm": c32.take(NF // 16), "sc": c32.take(NF // 16),
                "rs": c32.take(NF // 16)}
    qh_a = c32.take(NF // 2).bitcast(F16)
    qh_b = c32.take(NF // 2).bitcast(F16)
    p3a_scbf = {"shb": c32.take(NF // 32 + 8).bitcast(BF16),
                "smb": c32.take(NF // 32 + 8).bitcast(BF16)}
    for _mi in range(4):
        p3a_scbf[f"m{_mi}"] = c32.take(NF // 2).bitcast(BF16)
    qxTa = [c16.take(512) for _ in range(KC)]
    qxTb = [c16.take(512) for _ in range(KC)]
    qwt0 = {nm: [c16.take(512) for _ in range(KC)]
            for nm in ("ga", "gb", "ua", "ub")}
    abf16 = ABF.bitcast(F16)
    qwt1 = {nm: [abf16[:, (j * KC + k) * 512:(j * KC + k + 1) * 512]
                 for k in range(KC)]
            for j, nm in enumerate(("ga", "gb", "ua", "ub"))}
    qwts = [qwt0, qwt1]
    mm_done_par = [0, 0]
    h_evac = [0, 0, 0, 0]
    p3a_pst = [0]
    wdw = [h_t[0][:, :NF], h_t[1][:, :NF]]
    wd_pv = [cnt["V"], cnt["V"]]
    wd_pst = 0
    for i in range(DM // 128):
        p = i % 2
        rows = slice(i * 128, (i + 1) * 128)
        la = dma("scalar", wdw[p], wd_d[rows, :], sem="LB",
                 waits=[("V", wd_pv[p])])
        v = emit_qd(wdw[p], NF, qh_a, qh_b, p3a_sc32, p3a_scbf,
                    waits=[("LB", la), ("SG", wd_pst)])
        dma("gpsimd", qwda_d[rows, :], qh_a, sem="SG", waits=[("V", v)])
        wd_pst = dma("gpsimd", qwdb_d[rows, :], qh_b, sem="SG", waits=[("V", v)])
        wd_pv[p] = v
    p3a_pst[0] = wd_pst
    for tb in range(4):
        t0 = tb * 512
        lx = 0
        for k in range(KC):
            w = [("ST", cnt["ST"]), ("T", cnt["T"])] if k == 0 else []
            lx = dma("sync", qxTa[k], qxa_d[t0:t0 + 512, k * 128:(k + 1) * 128],
                     sem="LA", waits=w, transpose=True)
            lx = dma("sync", qxTb[k], qxb_d[t0:t0 + 512, k * 128:(k + 1) * 128],
                     sem="LA", transpose=True)
        for ni, (n0, nsz) in enumerate(zip(NOFFS, NSIZES)):
            par = ni % 2
            qwt = qwts[par]
            lw = 0
            for k in range(KC):
                kcols = slice(k * 128, (k + 1) * 128)
                nrows = slice(n0, n0 + nsz)
                w = [("T", mm_done_par[par])] if k == 0 else []
                lw = dma("sync", qwt["ga"][k][:, :nsz], qwga_d[nrows, kcols],
                         sem="LA", waits=w, transpose=True)
                lw = dma("sync", qwt["gb"][k][:, :nsz], qwgb_d[nrows, kcols],
                         sem="LA", transpose=True)
                lw = dma("sync", qwt["ua"][k][:, :nsz], qwua_d[nrows, kcols],
                         sem="LA", transpose=True)
                lw = dma("sync", qwt["ub"][k][:, :nsz], qwub_d[nrows, kcols],
                         sem="LA", transpose=True)
            for tt in range(4):
                bg, bu = (tt * 2) % 8, (tt * 2 + 1) % 8
                mm = []
                for k in range(KC):
                    sa = qxTa[k][:, tt * 128:(tt + 1) * 128]
                    sb = qxTb[k][:, tt * 128:(tt + 1) * 128]
                    fst, lst = (k == 0), (k == KC - 1)
                    mm.append(lambda e, _s=sa, _r=qwt["ga"][k][:, :nsz], _b=bg, _f=fst, _n=nsz:
                              e.matmul(psum[_b][:, :_n], _s, _r, start=_f, stop=False))
                    mm.append(lambda e, _s=sa, _r=qwt["ua"][k][:, :nsz], _b=bu, _f=fst, _n=nsz:
                              e.matmul(psum[_b][:, :_n], _s, _r, start=_f, stop=False))
                    mm.append(lambda e, _s=sa, _r=qwt["gb"][k][:, :nsz], _b=bg, _n=nsz:
                              e.matmul(psum[_b][:, :_n], _s, _r, start=False, stop=False))
                    mm.append(lambda e, _s=sa, _r=qwt["ub"][k][:, :nsz], _b=bu, _n=nsz:
                              e.matmul(psum[_b][:, :_n], _s, _r, start=False, stop=False))
                    mm.append(lambda e, _s=sb, _r=qwt["ga"][k][:, :nsz], _b=bg, _l=lst, _n=nsz:
                              e.matmul(psum[_b][:, :_n], _s, _r, start=False, stop=_l))
                    mm.append(lambda e, _s=sb, _r=qwt["ua"][k][:, :nsz], _b=bu, _l=lst, _n=nsz:
                              e.matmul(psum[_b][:, :_n], _s, _r, start=False, stop=_l))
                tmm = step("tensor", "T", [("LA", lw),
                                           bank_free[bg], bank_free[bu]], mm)
                mm_done_par[par] = tmm
                se = step("scalar", "S", [("T", tmm), ("V", h_evac[tt])], [
                    lambda e, _b=bg, _n=nsz, _g=sg[tt]: e.activation(_g[:, :_n], psum[_b][:, :_n], AF.Silu),
                ])
                ve = step("vector", "V", [("S", se)], [
                    lambda e, _b=bu, _h=h_t[tt][:, n0:n0 + nsz], _n=nsz, _g=sg[tt]:
                        e.tensor_tensor(_h, psum[_b][:, :_n], _g[:, :_n], ALU.mult),
                ])
                h_evac[tt] = ve
                bank_free[bg] = ("S", se)
                bank_free[bu] = ("V", ve)
        for tt in range(4):
            rows = slice(t0 + tt * 128, t0 + (tt + 1) * 128)
            v = emit_qd(h_t[tt], NF, qh_a, qh_b, p3a_sc32, p3a_scbf,
                        waits=[("V", h_evac[tt]), ("SG", p3a_pst[0])])
            dma("gpsimd", qha_d[rows, :], qh_a, sem="SG", waits=[("V", v)])
            p3a_pst[0] = dma("gpsimd", qhb_d[rows, :], qh_b, sem="SG", waits=[("V", v)])
            h_evac[tt] = v
    barrier()
    if stage == "p3a":
        return finish(nc, ops, sems, cnt)

    KQ = [list(range(0, 6)), list(range(6, 12)),
          list(range(12, 17)), list(range(17, 22))]
    for tb in range(2):
        c32.reset()
        t0 = tb * 1024
        facc = [c32.take(1024) for _ in range(8)]
        facc_v = [0] * 8
        for qi, kq in enumerate(KQ):
            c16.reset()
            qhTa = {k: c16.take(1024) for k in kq}
            qhTb = {k: c16.take(1024) for k in kq}
            wda = {k: c16.take(1024) for k in kq}
            wdb = {k: c16.take(1024) for k in kq}
            la = 0
            for j, k in enumerate(kq):
                kcols = slice(k * 128, (k + 1) * 128)
                w = [("T", cnt["T"]), ("SG", cnt["SG"])] if j == 0 else []
                la = dma("sync", qhTa[k], qha_d[t0:t0 + 1024, kcols],
                         sem="LA", waits=w, transpose=True)
                la = dma("sync", qhTb[k], qhb_d[t0:t0 + 1024, kcols],
                         sem="LA", transpose=True)
                la = dma("sync", wda[k], qwda_d[:, kcols], sem="LA",
                         transpose=True)
                la = dma("sync", wdb[k], qwdb_d[:, kcols], sem="LA",
                         transpose=True)
            for d in range(8):
                dcols = slice(d * 128, (d + 1) * 128)
                b0, b1 = (d * 2) % 8, (d * 2 + 1) % 8
                mm = []
                for j, k in enumerate(kq):
                    fst, lst = (j == 0), (j == len(kq) - 1)
                    for th in range(2):
                        tsl2 = slice(th * 512, (th + 1) * 512)
                        bb = b0 if th == 0 else b1
                        mm.append(lambda e, _w=wda[k][:, dcols], _r=qhTa[k][:, tsl2], _b=bb, _f=fst:
                                  e.matmul(psum[_b], _w, _r, start=_f, stop=False))
                        mm.append(lambda e, _w=wda[k][:, dcols], _r=qhTb[k][:, tsl2], _b=bb:
                                  e.matmul(psum[_b], _w, _r, start=False, stop=False))
                        mm.append(lambda e, _w=wdb[k][:, dcols], _r=qhTa[k][:, tsl2], _b=bb, _l=lst:
                                  e.matmul(psum[_b], _w, _r, start=False, stop=_l))
                tmm = step("tensor", "T", [("LA", la),
                                           bank_free[b0], bank_free[b1]], mm)
                if qi == 0:
                    ve = step("vector", "V", [("T", tmm)], [
                        lambda e, _b=b0, _f=facc[d][:, 0:512]: e.tensor_copy(_f, psum[_b]),
                        lambda e, _b=b1, _f=facc[d][:, 512:1024]: e.tensor_copy(_f, psum[_b]),
                    ])
                else:
                    ve = step("vector", "V", [("T", tmm), ("V", facc_v[d])], [
                        lambda e, _b=b0, _f=facc[d][:, 0:512]:
                            e.tensor_tensor(_f, _f, psum[_b], ALU.add),
                        lambda e, _b=b1, _f=facc[d][:, 512:1024]:
                            e.tensor_tensor(_f, _f, psum[_b], ALU.add),
                    ])
                facc_v[d] = ve
                bank_free[b0] = ("V", ve)
                bank_free[b1] = ("V", ve)
        for d in range(8):
            dma("gpsimd", ffn_d[d * 128:(d + 1) * 128, t0:t0 + 1024], facc[d],
                sem="SG", waits=[("V", facc_v[d])])
        barrier()
    if stage == "p3c":
        return finish(nc, ops, sems, cnt)

    c32.reset()
    c16.reset()
    cbf.reset()
    xbr = c32.take(DIC * 515)
    xcy = c32.take(DIC * 512)
    mmr = c32.take(DIC * 512)
    ssm_s = c32.take(512)
    dtb = c32.take(512)
    sp1 = c32.take(512)
    sp2 = c32.take(512)
    dAb = [c32.take(512), c32.take(512)]
    ub = [c32.take(512), c32.take(512)]
    hbf2 = [c32.take(512), c32.take(512)]
    tmpb = sp2
    Bsb = [c32.take(512), c32.take(512)]
    Csb = [dtb, ssm_s]
    szb = sp1

    xnTh = [c16.take(512) for _ in range(KC)]
    xnTl = [c16.take(512) for _ in range(KC)]
    wstah = [c16.take(2048), c16.take(2048)]
    wstal = [c16.take(2048), c16.take(2048)]
    y2h = c16.take(DIC * 512)
    y2l = ABF.bitcast(F16)[:, 0:DIC * 512]
    xchb = y2h[:, 0:512]
    xclb = y2h[:, 512:1024]

    def xb_t(i):
        return xbr[:, i * 515:(i + 1) * 515]

    def dt_t(i):
        return xbr[:, i * 512:(i + 1) * 512]

    def xc_t(i):
        return xcy[:, i * 512:(i + 1) * 512]

    def m_t(i):
        return mmr[:, i * 512:(i + 1) * 512]

    sum_t = [mmr[:, i * 512:(i + 1) * 512] for i in range(8)]
    outtm = mmr[:, 8 * 512:8 * 512 + 1024]
    xtm = mmr[:, 8 * 512 + 1024:8 * 512 + 2048]

    def load_sta(dst, src2d, kchunks, w):
        return dma("gpsimd",
                   dst[:, :kchunks * 128].rearrange("p (k n) -> p k n", n=128),
                   src2d.rearrange("(k p) n -> p k n", p=128),
                   sem="LG", waits=w)

    for ch in range(NCH):
        tsl = slice(ch * 512, (ch + 1) * 512)
        lxn = 0
        for k in range(KC):
            kcols = slice(k * 128, (k + 1) * 128)
            w = [("ST", cnt["ST"]), ("SG", cnt["SG"]), ("T", cnt["T"])] if k == 0 else []
            lxn = dma("sync", xnTh[k], xnh_d[tsl, kcols], sem="LA",
                      waits=w, transpose=True)
            lxn = dma("sync", xnTl[k], xnl_d[tsl, kcols], sem="LA",
                      transpose=True)
        s_cv = []
        win_mm_par = [0, 0]
        for n in range(DIC):
            par = n % 2
            ncols = slice(n * 128, (n + 1) * 128)
            load_sta(wstah[par], winh_d[:, ncols], KC, [("T", win_mm_par[par])])
            lw = load_sta(wstal[par], winl_d[:, ncols], KC, [])
            bnk = n % 4
            mm = []
            for k in range(KC):
                fst, lst = (k == 0), (k == KC - 1)
                wh = wstah[par][:, k * 128:(k + 1) * 128]
                wl = wstal[par][:, k * 128:(k + 1) * 128]
                mm.append(lambda e, _w=wh, _r=xnTh[k], _b=bnk, _f=fst:
                          e.matmul(psum[_b], _w, _r, start=_f, stop=False))
                mm.append(lambda e, _w=wh, _r=xnTl[k], _b=bnk:
                          e.matmul(psum[_b], _w, _r, start=False, stop=False))
                mm.append(lambda e, _w=wl, _r=xnTh[k], _b=bnk, _l=lst:
                          e.matmul(psum[_b], _w, _r, start=False, stop=_l))
            tmm = step("tensor", "T", [("LG", lw), ("LA", lxn), bank_free[bnk]], mm)
            win_mm_par[par] = tmm
            s_ev = step("scalar", "S", [("T", tmm)], [
                lambda e, _b=bnk, _x=xb_t(n): e.activation(_x[:, 3:515], psum[_b], AF.Copy),
            ])
            bank_free[bnk] = ("S", s_ev)
            halo3 = halo_s[:, n * 3:(n + 1) * 3]
            if ch == 0:
                pre = [lambda e, _x=xb_t(n): e.memset(_x[:, 0:3], 0.0)]
            else:
                pre = [lambda e, _x=xb_t(n), _h=halo3: e.tensor_copy(_x[:, 0:3], _h)]
            vcv = step("vector", "V", [("S", s_ev)], pre + [
                lambda e, _x=xb_t(n), _c=cw_s[:, n * 4:n * 4 + 1]:
                    e.tensor_scalar(sp1, _x[:, 0:512], _c, None, ALU.mult),
                lambda e, _x=xb_t(n), _c=cw_s[:, n * 4 + 1:n * 4 + 2]:
                    e.scalar_tensor_tensor(sp2, _x[:, 1:513], _c, sp1, ALU.mult, ALU.add),
                lambda e, _x=xb_t(n), _c=cw_s[:, n * 4 + 2:n * 4 + 3]:
                    e.scalar_tensor_tensor(sp1, _x[:, 2:514], _c, sp2, ALU.mult, ALU.add),
                lambda e, _x=xb_t(n), _c=cw_s[:, n * 4 + 3:n * 4 + 4]:
                    e.scalar_tensor_tensor(sp2, _x[:, 3:515], _c, sp1, ALU.mult, ALU.add),
                lambda e, _x=xb_t(n), _h=halo3: e.tensor_copy(_h, _x[:, 512:515]),
            ])
            sc_ = step("scalar", "S", [("V", vcv)], [
                lambda e, _o=xc_t(n), _b=cb_s[:, n:n + 1]:
                    e.activation(_o, sp2, AF.Silu, bias=_b),
            ])
            s_cv.append(sc_)
            if "s_xb" in dbg:
                dma("sync", xb_dbg[n * 128:(n + 1) * 128, tsl], xb_t(n)[:, 3:515],
                    sem="ST", waits=[("V", vcv)])
            if "s_xc" in dbg:
                dma("sync", xc_dbg[n * 128:(n + 1) * 128, tsl], xc_t(n),
                    sem="ST", waits=[("S", sc_)])
        prev_wx_mm = 0
        for k in range(DIC):
            vch = step("vector", "V", [("S", s_cv[k]), ("T", prev_wx_mm)], [
                lambda e, _x=xc_t(k): e.tensor_copy(xchb, _x),
                lambda e, _x=xc_t(k): e.tensor_tensor(xclb, _x, xchb, ALU.subtract),
            ])
            fst, lst = (k == 0), (k == DIC - 1)
            wcols = slice(k * 33, (k + 1) * 33)
            mm = [
                lambda e, _w=wxh_s[:, wcols], _f=fst:
                    e.matmul(psum[4][0:33, :], _w, xchb, start=_f, stop=False),
                lambda e, _w=wxh_s[:, wcols]:
                    e.matmul(psum[4][0:33, :], _w, xclb, start=False, stop=False),
                lambda e, _w=wxl_s[:, wcols], _l=lst:
                    e.matmul(psum[4][0:33, :], _w, xchb, start=False, stop=_l),
            ]
            wts = [("V", vch)]
            if k == 0:
                wts.append(bank_free[4])
            tmm = step("tensor", "T", wts, mm)
            prev_wx_mm = tmm
        s_sx = step("scalar", "S", [("T", cnt["T"])], [
            lambda e: e.activation(ssm_s[0:33, :], psum[4][0:33, :], AF.Copy),
        ])
        bank_free[4] = ("S", s_sx)
        st_ssm = dma("sync", ssm_d[:, tsl], ssm_s[0:33, :], sem="ST",
                     waits=[("S", s_sx)])
        l_dtb = dma("sync", dtb, ssm_d[32:33, tsl].partition_broadcast(128),
                    sem="LA", waits=[("ST", st_ssm)])
        vdt_prev = 0
        for i in range(DIC):
            sdt = step("scalar", "S", [("LA", l_dtb), ("V", vdt_prev)], [
                lambda e, _b=db_s[:, i:i + 1]: e.activation(sp1, dtb, AF.Abs, bias=_b),
                lambda e: e.activation(sp2, sp1, AF.Exp, scale=csts(IC_NEG1)),
                lambda e: e.activation(sp1, sp2, AF.Ln, bias=csts(IC_1)),
            ])
            vdt_prev = step("vector", "V", [("S", sdt)], [
                lambda e, _b=db_s[:, i:i + 1]:
                    e.tensor_scalar(sp2, dtb, _b, 0.0, ALU.add, ALU.max),
                lambda e, _d=dt_t(i): e.tensor_tensor(_d, sp1, sp2, ALU.add),
            ])
        if "s_dt" in dbg:
            for i in range(DIC):
                dma("sync", dt_dbg[i * 128:(i + 1) * 128, tsl], dt_t(i),
                    sem="ST", waits=[("V", cnt["V"])])
        for i in range(DIC):
            step("vector", "V", [], [
                lambda e, _m=m_t(i), _d=dt_t(i), _x=xc_t(i):
                    e.tensor_tensor(_m, _d, _x, ALU.mult),
                lambda e, _x=xc_t(i), _p=dp_s[:, i:i + 1]:
                    e.tensor_scalar(_x, _x, _p, None, ALU.mult),
            ])
        VB, SB, GB = cnt["V"], cnt["S"], cnt["G"]
        bs_val = {}
        cs_val = {}
        for s in range(S):
            w = [] if s < 2 else [("V", VB + (s - 1) * DIC)]
            wc = [("V", VB), ("ST", st_ssm)] if s < 2 else w
            bs_val[s] = dma("sync", Bsb[s % 2],
                            ssm_d[s:s + 1, tsl].partition_broadcast(128),
                            sem="LA", waits=w)
            cs_val[s] = dma("sync", Csb[s % 2],
                            ssm_d[S + s:S + s + 1, tsl].partition_broadcast(128),
                            sem="LA", waits=wc)
        prev_col = None
        for s in range(S):
            for ci in range(DIC):
                i = s * DIC + ci
                col = ci * S + s
                step("scalar", "S", [("V", max(VB, VB + i - 1))], [
                    lambda e, _o=dAb[i % 2], _d=dt_t(ci), _a=a_s[:, col:col + 1]:
                        e.activation(_o, _d, AF.Exp, scale=_a),
                ])
                step("gpsimd", "G",
                     [("V", max(VB, VB + i - 1)), ("LA", bs_val[s])], [
                    lambda e, _o=ub[i % 2], _m=m_t(ci), _b=Bsb[s % 2]:
                        e.tensor_tensor(_o, _m, _b, ALU.mult),
                ])
                bl = [
                    lambda e, _a=dAb[i % 2], _u=ub[i % 2], _o=hbf2[i % 2],
                           _h=hcar_s[:, col:col + 1]:
                        e.tensor_tensor_scan(_o, _a, _u, _h, ALU.mult, ALU.add),
                ]
                if prev_col is not None:
                    bl.append(lambda e, _h=hcar_s[:, prev_col:prev_col + 1],
                              _s=hbf2[(i - 1) % 2]:
                              e.tensor_copy(_h, _s[:, 511:512]))
                bl += [
                    lambda e, _c=Csb[s % 2], _o=hbf2[i % 2]:
                        e.tensor_tensor(tmpb, _o, _c, ALU.mult),
                    lambda e, _y=xc_t(ci):
                        e.tensor_tensor(_y, _y, tmpb, ALU.add),
                ]
                step("vector", "V",
                     [("S", SB + i + 1), ("G", GB + i + 1), ("LA", cs_val[s])], bl)
                prev_col = col
        step("vector", "V", [], [
            lambda e, _h=hcar_s[:, prev_col:prev_col + 1],
                   _s=hbf2[(S * DIC - 1) % 2]:
                (e.drain(), e.tensor_copy(_h, _s[:, 511:512]))[1],
        ])
        if "s_y" in dbg:
            for i in range(DIC):
                dma("sync", y_dbg[i * 128:(i + 1) * 128, tsl], xc_t(i),
                    sem="ST", waits=[("V", cnt["V"])])
        v_y2p = 0
        z_mm_par = [cnt["T"], cnt["T"]]
        for i in range(DIC):
            par = i % 2
            zcols = slice(DI + i * 128, DI + (i + 1) * 128)
            load_sta(wstah[par], winh_d[:, zcols], KC, [("T", z_mm_par[par])])
            lw = load_sta(wstal[par], winl_d[:, zcols], KC, [])
            bnk = i % 4
            mm = []
            for k in range(KC):
                fst, lst = (k == 0), (k == KC - 1)
                wh = wstah[par][:, k * 128:(k + 1) * 128]
                wl = wstal[par][:, k * 128:(k + 1) * 128]
                mm.append(lambda e, _w=wh, _r=xnTh[k], _b=bnk, _f=fst:
                          e.matmul(psum[_b], _w, _r, start=_f, stop=False))
                mm.append(lambda e, _w=wh, _r=xnTl[k], _b=bnk:
                          e.matmul(psum[_b], _w, _r, start=False, stop=False))
                mm.append(lambda e, _w=wl, _r=xnTh[k], _b=bnk, _l=lst:
                          e.matmul(psum[_b], _w, _r, start=False, stop=_l))
            tmm = step("tensor", "T", [("LG", lw), bank_free[bnk]], mm)
            z_mm_par[par] = tmm
            s_z = step("scalar", "S", [("T", tmm), ("V", v_y2p)], [
                lambda e, _b=bnk: e.activation(szb, psum[_b], AF.Silu),
            ])
            bank_free[bnk] = ("S", s_z)
            v_y2p = step("vector", "V", [("S", s_z)], [
                lambda e, _y=xc_t(i): e.tensor_tensor(_y, _y, szb, ALU.mult),
                lambda e, _y=xc_t(i), _o=y2h[:, i * 512:(i + 1) * 512]:
                    e.tensor_copy(_o, _y),
                lambda e, _y=xc_t(i), _h=y2h[:, i * 512:(i + 1) * 512],
                       _o=y2l[:, i * 512:(i + 1) * 512]:
                    e.tensor_tensor(_o, _y, _h, ALU.subtract),
            ])
        v_wo = [0] * 8
        wo_mm_par = [cnt["T"], cnt["T"]]
        for no in range(8):
            par = no % 2
            ncols = slice(no * 128, (no + 1) * 128)
            load_sta(wstah[par], wouth_d[:, ncols], DIC, [("T", wo_mm_par[par])])
            lw = load_sta(wstal[par], woutl_d[:, ncols], DIC, [])
            fl = dma("gpsimd", dAb[no % 2], ffn_d[ncols, tsl], sem="LG",
                     waits=[("V", v_wo[no - 2] if no >= 2 else cnt["V"])])
            bnk = 4 + no % 4
            mm = []
            for k in range(DIC):
                fst, lst = (k == 0), (k == DIC - 1)
                wh = wstah[par][:, k * 128:(k + 1) * 128]
                wl = wstal[par][:, k * 128:(k + 1) * 128]
                y2hs = y2h[:, k * 512:(k + 1) * 512]
                y2ls = y2l[:, k * 512:(k + 1) * 512]
                mm.append(lambda e, _w=wh, _r=y2hs, _b=bnk, _f=fst:
                          e.matmul(psum[_b], _w, _r, start=_f, stop=False))
                mm.append(lambda e, _w=wh, _r=y2ls, _b=bnk:
                          e.matmul(psum[_b], _w, _r, start=False, stop=False))
                mm.append(lambda e, _w=wl, _r=y2hs, _b=bnk, _l=lst:
                          e.matmul(psum[_b], _w, _r, start=False, stop=_l))
            tmm = step("tensor", "T", [("LG", max(lw, fl)), ("V", v_y2p), bank_free[bnk]], mm)
            wo_mm_par[par] = tmm
            v_wo[no] = step("vector", "V", [("T", tmm), ("LG", fl)], [
                lambda e, _b=bnk, _s=sum_t[no], _f=dAb[no % 2]:
                    e.tensor_tensor(_s, psum[_b], _f, ALU.add),
            ])
            bank_free[bnk] = ("V", v_wo[no])
        st_out = 0
        v_as = 0
        for tsub in range(4):
            rows = slice(ch * 512 + tsub * 128, ch * 512 + (tsub + 1) * 128)
            lxr = dma("gpsimd", xtm, x_d[rows, :], sem="LG",
                      waits=[("V", v_as)])
            for no in range(8):
                bnk = no % 2
                ttr = step("tensor", "T",
                           [bank_free[bnk], ("V", v_wo[no])], [
                    lambda e, _s=sum_t[no][:, tsub * 128:(tsub + 1) * 128], _b=bnk:
                        e.transpose(psum[_b][:, 0:128], _s, idn_s),
                ])
                v_as = step("vector", "V",
                            [("T", ttr), ("LG", lxr), ("SG", st_out)], [
                    lambda e, _b=bnk, _o=outtm[:, no * 128:(no + 1) * 128],
                           _x=xtm[:, no * 128:(no + 1) * 128]:
                        e.tensor_tensor(_o, psum[_b][:, 0:128], _x, ALU.add),
                ])
                bank_free[bnk] = ("V", v_as)
            st_out = dma("gpsimd", out_d[rows, :], outtm, sem="SG",
                         waits=[("V", v_as)])
        barrier()

    return finish(nc, ops, sems, cnt)


def finish(nc, ops, sems, cnt):
    import contextlib

    final = [(s, cnt[s]) for s in ("ST", "SG", "LA", "LB", "LG") if cnt[s] > 0]

    def tail(e, _f=tuple(final)):
        for s2, val in _f:
            e.wait_ge(sems[s2], val)
    ops["sync"].append(tail)

    with contextlib.ExitStack() as st:
        for n in ["LA", "LB", "LG", "T", "V", "S", "G", "ST", "SG"]:
            sems[n] = st.enter_context(nc.semaphore(n))
        block = st.enter_context(nc.Block())
        for eng in ENGS:
            fns = tuple(ops[eng])

            def body(e, _fns=fns):
                for f in _fns:
                    f(e)
            getattr(block, eng)(body)
    return nc


def _prep_maps(inputs):
    f32 = np.float32
    x = np.ascontiguousarray(inputs["L0_attn_x"], dtype=f32)
    win = np.ascontiguousarray(inputs["L0_attn_W_in"], dtype=f32)
    wout = np.ascontiguousarray(inputs["L0_attn_W_out"], dtype=f32)
    wg = np.ascontiguousarray(inputs["L0_ffn_w_gate"], dtype=f32)
    wu = np.ascontiguousarray(inputs["L0_ffn_w_up"], dtype=f32)
    wd = np.ascontiguousarray(inputs["L0_ffn_w_down"], dtype=f32)
    alog = np.ascontiguousarray(
        np.asarray(inputs["L0_attn_A_log"], dtype=f32)
        .reshape(DIC, 128, S).transpose(1, 0, 2).reshape(128, DIC * S))
    db = np.ascontiguousarray(
        np.asarray(inputs["L0_attn_dt_bias"], dtype=f32).reshape(DIC, 128).T)
    dp = np.ascontiguousarray(
        np.asarray(inputs["L0_attn_D_param"], dtype=f32).reshape(DIC, 128).T)
    cb = np.ascontiguousarray(
        np.asarray(inputs["L0_attn_conv_b"], dtype=f32).reshape(DIC, 128).T)
    cw = np.ascontiguousarray(
        np.asarray(inputs["L0_attn_conv_w"], dtype=f32)[:, 0, :]
        .reshape(DIC, 128, 4).transpose(1, 0, 2).reshape(128, DIC * 4))
    wx = np.ascontiguousarray(
        np.asarray(inputs["L0_attn_W_x"], dtype=f32)
        .reshape(DIC, 128, 33).transpose(1, 0, 2).reshape(128, DIC * 33))
    cst = np.tile(np.asarray(CST, dtype=f32)[None, :], (128, 1))
    idn = np.eye(128, dtype=f32)
    shared = dict(w_in=win, w_out=wout, w_gate=wg, w_up=wu, w_down=wd,
                  a_log=alog, dt_bias=db, d_param=dp, conv_b=cb, conv_w=cw,
                  w_x=wx, cst=cst, idn=idn)
    maps = []
    for i in range(8):
        m = dict(shared)
        m["x"] = np.ascontiguousarray(x[i])
        maps.append(m)
    return maps


def kernel(**inputs):
    nc = build()
    maps = _prep_maps(inputs)
    res = run_bass_kernel_spmd(nc, maps, core_ids=list(range(8)))
    out = np.stack([np.asarray(r["out"]) for r in res.results], axis=0)
    return out.astype(np.float32)


# version 13
# speedup vs baseline: 1.1304x; 1.0633x over previous
import numpy as np
import concourse.bass as bass
import concourse.mybir as mybir
from concourse.bass_utils import run_bass_kernel_spmd

F32 = mybir.dt.float32
F16 = mybir.dt.float16
BF16 = mybir.dt.bfloat16
AF = mybir.ActivationFunctionType
ALU = mybir.AluOpType
AX = mybir.AxisListType

T = 2048
DM = 1024
DI = 2048
S = 16
NF = 2816
TC = 512
NCH = T // TC
KC = DM // 128
DIC = DI // 128
NFC = NF // 128

CST = [-1.0, 1.0, -0.25, -0.75, -1.25, -1.75]
IC_NEG1, IC_1, IC_M25, IC_M75, IC_M125, IC_M175 = range(6)

ENGS = ("sync", "tensor", "vector", "scalar", "gpsimd")


def build(stage="full", dbg=()):
    nc = bass.Bass("TRN2", target_bir_lowering=False)

    def dram(name, shape, dtype):
        kind = "ExternalOutput" if name in dbg else "Internal"
        return nc.dram_tensor(name, shape, dtype, kind=kind).ap()

    x_d = nc.dram_tensor("x", [T, DM], F32, kind="ExternalInput").ap()
    win_d = nc.dram_tensor("w_in", [DM, 2 * DI], F32, kind="ExternalInput").ap()
    wout_d = nc.dram_tensor("w_out", [DI, DM], F32, kind="ExternalInput").ap()
    wg_d = nc.dram_tensor("w_gate", [NF, DM], F32, kind="ExternalInput").ap()
    wu_d = nc.dram_tensor("w_up", [NF, DM], F32, kind="ExternalInput").ap()
    wd_d = nc.dram_tensor("w_down", [DM, NF], F32, kind="ExternalInput").ap()
    alog_d = nc.dram_tensor("a_log", [128, DIC * S], F32, kind="ExternalInput").ap()
    db_d = nc.dram_tensor("dt_bias", [128, DIC], F32, kind="ExternalInput").ap()
    dp_d = nc.dram_tensor("d_param", [128, DIC], F32, kind="ExternalInput").ap()
    cb_d = nc.dram_tensor("conv_b", [128, DIC], F32, kind="ExternalInput").ap()
    cw_d = nc.dram_tensor("conv_w", [128, DIC * 4], F32, kind="ExternalInput").ap()
    wx_d = nc.dram_tensor("w_x", [128, DIC * 33], F32, kind="ExternalInput").ap()
    cst_d = nc.dram_tensor("cst", [128, len(CST)], F32, kind="ExternalInput").ap()
    idn_d = nc.dram_tensor("idn", [128, 128], F32, kind="ExternalInput").ap()
    out_d = nc.dram_tensor("out", [T, DM], F32, kind="ExternalOutput").ap()

    winh_d = dram("s_winh", [DM, 2 * DI], F16)
    winl_d = dram("s_winl", [DM, 2 * DI], F16)
    wouth_d = dram("s_wouth", [DI, DM], F16)
    woutl_d = dram("s_woutl", [DI, DM], F16)
    qwga_d = dram("s_qwga", [NF, DM], F16)
    qwgb_d = dram("s_qwgb", [NF, DM], F16)
    qwua_d = dram("s_qwua", [NF, DM], F16)
    qwub_d = dram("s_qwub", [NF, DM], F16)
    qwda_d = dram("s_qwda", [DM, NF], F16)
    qwdb_d = dram("s_qwdb", [DM, NF], F16)
    xnh_d = dram("s_xnh", [T, DM], F16)
    xnl_d = dram("s_xnl", [T, DM], F16)
    qxa_d = dram("s_qxa", [T, DM], F16)
    qxb_d = dram("s_qxb", [T, DM], F16)
    ssm_d = dram("s_ssm", [33, T], F32)
    qha_d = dram("s_qha", [T, NF], F16)
    qhb_d = dram("s_qhb", [T, NF], F16)
    ffn_d = dram("s_ffn", [DM, T], F32)
    xb_dbg = dram("s_xb", [DI, T], F32)
    xc_dbg = dram("s_xc", [DI, T], F32)
    dt_dbg = dram("s_dt", [DI, T], F32)
    y_dbg = dram("s_y", [DI, T], F32)

    cst_s = nc.alloc_sbuf_tensor("cst_s", [128, len(CST)], F32).ap()
    idn_s = nc.alloc_sbuf_tensor("idn_s", [128, 128], F32).ap()
    a_s = nc.alloc_sbuf_tensor("a_s", [128, DIC * S], F32).ap()
    db_s = nc.alloc_sbuf_tensor("db_s", [128, DIC], F32).ap()
    dp_s = nc.alloc_sbuf_tensor("dp_s", [128, DIC], F32).ap()
    cb_s = nc.alloc_sbuf_tensor("cb_s", [128, DIC], F32).ap()
    cw_s = nc.alloc_sbuf_tensor("cw_s", [128, DIC * 4], F32).ap()
    wxh_s = nc.alloc_sbuf_tensor("wxh_s", [128, DIC * 33], F16).ap()
    wxl_s = nc.alloc_sbuf_tensor("wxl_s", [128, DIC * 33], F16).ap()
    hcar_s = nc.alloc_sbuf_tensor("hcar_s", [128, DIC * S], F32).ap()
    halo_s = nc.alloc_sbuf_tensor("halo_s", [128, DIC * 3], F32).ap()

    AF32 = nc.alloc_sbuf_tensor("AF32", [128, 30976], F32).ap()
    AF16 = nc.alloc_sbuf_tensor("AF16", [128, 24576], F16).ap()
    ABF = nc.alloc_sbuf_tensor("ABF", [128, 16384], BF16).ap()

    psum = [nc.alloc_psum_tensor(f"pp{i}", [128, 512], F32).ap() for i in range(8)]

    class Carver:
        def __init__(self, ap, total):
            self.ap, self.total, self.off = ap, total, 0

        def take(self, cols):
            assert self.off + cols <= self.total, (self.off, cols, self.total)
            r = self.ap[:, self.off:self.off + cols]
            self.off += cols
            return r

        def reset(self):
            self.off = 0

    c32 = Carver(AF32, 30976)
    c16 = Carver(AF16, 24576)
    cbf = Carver(ABF, 16384)

    from collections import defaultdict
    ops = {e: [] for e in ENGS}
    cnt = defaultdict(int)
    SEMNAMES = ["LA", "LB", "LG", "T", "V", "S", "G", "ST", "SG", "SC"]
    sems = {}

    def emit(eng, f):
        ops[eng].append(f)

    def dma(qeng, out_ap, in_ap, sem="LA", waits=(), transpose=False):
        cnt[sem] += 16
        v = cnt[sem]
        w = tuple((s2, val) for (s2, val) in waits if val > 0)

        def f(e, _o=out_ap, _i=in_ap, _s=sem, _w=w, _t=transpose):
            for (s2, val) in _w:
                e.wait_ge(sems[s2], val)
            kw = {"transpose": True} if _t else {}
            e.dma_start(out=_o, in_=_i, **kw).then_inc(sems[_s], 16)
        emit(qeng, f)
        return v

    def step(eng, sem, waits, builders):
        cnt[sem] += 1
        v = cnt[sem]
        w = tuple((s2, val) for (s2, val) in waits if val > 0)
        bs = tuple(builders)
        assert bs

        def f(e, _w=w, _b=bs, _s=sem):
            for (s2, val) in _w:
                e.wait_ge(sems[s2], val)
            for b in _b[:-1]:
                b(e)
            _b[-1](e).then_inc(sems[_s], 1)
        emit(eng, f)
        return v

    def waitonly(eng, waits):
        w = tuple((s2, val) for (s2, val) in waits if val > 0)

        def f(e, _w=w):
            for (s2, val) in _w:
                e.wait_ge(sems[s2], val)
        emit(eng, f)

    def barrier():
        vals = [(s, cnt[s]) for s in ("T", "V", "S", "G", "LA", "LB", "LG", "ST", "SG", "SC")]
        for eng in ENGS:
            waitonly(eng, vals)

    def csts(i):
        return cst_s[:, i:i + 1]

    bank_free = [("V", 0)] * 8

    def emit_qd(src, F, qa, qb, sc32, scbf, waits):
        B = F // 16
        w0 = sc32["w0"][:, :F]
        vv = sc32["vv"][:, :F]
        bm = sc32["bm"][:, :B]
        sc = sc32["sc"][:, :B]
        rs = sc32["rs"][:, :B]
        shb = scbf["shb"][:, :B]
        smb = scbf["smb"][:, :B]
        mk = [scbf[f"m{i}"][:, :F] for i in range(4)]

        srcg = src.rearrange("p (n b) -> p n b", b=16)
        w0g = w0.rearrange("p (n b) -> p n b", b=16)
        qag = qa.rearrange("p (n b) -> p n b", b=16)
        qbg = qb.rearrange("p (n b) -> p n b", b=16)

        v1 = step("vector", "V", waits, [
            lambda e: e.tensor_reduce(bm, srcg, AX.X, ALU.max, apply_absolute_value=True),
            lambda e: (e.drain(), e.tensor_scalar(sc, bm, 1.0 / 6.0, 1e-12, ALU.mult, ALU.max))[1],
            lambda e: (e.drain(), e.reciprocal(rs, sc))[1],
            lambda e: e.tensor_copy(shb, sc),
            lambda e: (e.drain(), e.tensor_tensor(smb, sc, shb, ALU.subtract))[1],
            lambda e: e.tensor_tensor(w0g, srcg,
                                      rs.unsqueeze(2).broadcast_to([128, B, 16]),
                                      ALU.mult),
        ])
        s1 = step("scalar", "S", [("V", v1)], [
            lambda e: e.activation(vv, w0, AF.Abs),
            lambda e: e.activation(mk[0], vv, AF.Sign, bias=csts(IC_M25)),
            lambda e: e.activation(mk[1], vv, AF.Sign, bias=csts(IC_M75)),
            lambda e: e.activation(mk[2], vv, AF.Sign, bias=csts(IC_M125)),
        ])
        v2 = step("vector", "V", [("S", s1)], [
            lambda e: e.tensor_tensor(mk[0], mk[0], mk[1], ALU.add),
        ])
        s2 = step("scalar", "S", [("V", v2)], [
            lambda e: e.activation(mk[1], vv, AF.Sign, bias=csts(IC_M175)),
            lambda e: e.activation(mk[3], w0, AF.Sign),
        ])
        v3 = step("vector", "V", [("S", s2)], [
            lambda e: e.tensor_tensor(mk[1], mk[1], mk[2], ALU.add),
            lambda e: e.tensor_tensor(mk[0], mk[0], mk[1], ALU.add),
            lambda e: e.tensor_scalar(mk[1], vv, 2.5, None, ALU.is_gt),
            lambda e: e.tensor_scalar(mk[2], vv, 3.5, None, ALU.is_gt),
            lambda e: e.tensor_tensor(mk[1], mk[1], mk[2], ALU.add),
            lambda e: e.tensor_scalar(mk[2], vv, 5.0, None, ALU.is_gt),
            lambda e: e.tensor_scalar(mk[2], mk[2], 2.0, None, ALU.mult),
            lambda e: e.tensor_tensor(mk[1], mk[1], mk[2], ALU.add),
            lambda e: e.tensor_scalar(mk[0], mk[0], 0.25, 1.0, ALU.mult, ALU.add),
            lambda e: e.tensor_tensor(mk[0], mk[0], mk[1], ALU.add),
            lambda e: e.tensor_tensor(mk[0], mk[0], mk[3], ALU.mult),
            lambda e: e.tensor_tensor(qag, mk[0].rearrange("p (n b) -> p n b", b=16),
                                      shb.unsqueeze(2).broadcast_to([128, B, 16]),
                                      ALU.mult),
            lambda e: e.tensor_tensor(qbg, mk[0].rearrange("p (n b) -> p n b", b=16),
                                      smb.unsqueeze(2).broadcast_to([128, B, 16]),
                                      ALU.mult),
        ])
        return v3

    def qd_scratch(K):
        sc32 = {"w0": c32.take(K), "vv": c32.take(K),
                "bm": c32.take(K // 16), "sc": c32.take(K // 16),
                "rs": c32.take(K // 16)}
        scbf = {"shb": cbf.take(K // 16), "smb": cbf.take(K // 16)}
        for i in range(4):
            scbf[f"m{i}"] = cbf.take(K)
        return sc32, scbf

    c32.reset()
    wxf = c32.take(DIC * 33)
    dma("sync", cst_s, cst_d)
    dma("sync", idn_s, idn_d)
    dma("sync", a_s, alog_d)
    dma("sync", db_s, db_d)
    dma("sync", dp_s, dp_d)
    dma("sync", cb_s, cb_d)
    dma("sync", cw_s, cw_d)
    l1 = dma("sync", wxf, wx_d)
    sP0 = step("scalar", "S", [("LA", l1)], [
        lambda e: e.activation(a_s, a_s, AF.Exp),
    ])
    step("vector", "V", [("S", sP0)], [
        lambda e: e.tensor_scalar(a_s, a_s, -1.0, None, ALU.mult),
        lambda e: e.tensor_copy(wxh_s, wxf),
        lambda e: e.tensor_tensor(wxl_s, wxf, wxh_s, ALU.subtract),
        lambda e: e.memset(hcar_s, 0.0),
        lambda e: e.memset(halo_s, 0.0),
    ])
    barrier()

    def split_stream(src_d, dst_h, dst_l, nrows, K):
        c32.reset()
        c16.reset()
        wf = [c32.take(K), c32.take(K)]
        hh = [c16.take(K), c16.take(K)]
        ll = [c16.take(K), c16.take(K)]
        prev_v = [0, 0]
        prev_st = [0, 0]
        for i in range(nrows // 128):
            p = i % 2
            rows = slice(i * 128, (i + 1) * 128)
            lg = dma("gpsimd", hh[p], src_d[rows, :], sem="LG",
                     waits=[("ST", prev_st[p]), ("V", prev_v[p])])
            la = dma("sync", wf[p], src_d[rows, :], sem="LA",
                     waits=[("V", prev_v[p])])
            v = step("vector", "V", [("LG", lg), ("LA", la)], [
                lambda e, _l=ll[p], _w=wf[p], _h=hh[p]:
                    e.tensor_tensor(_l, _w, _h, ALU.subtract),
            ])
            dma("sync", dst_h[rows, :], hh[p], sem="ST", waits=[("LG", lg)])
            st2 = dma("sync", dst_l[rows, :], ll[p], sem="ST", waits=[("V", v)])
            prev_v[p] = v
            prev_st[p] = st2

    split_stream(win_d, winh_d, winl_d, DM, 2 * DI)
    split_stream(wout_d, wouth_d, woutl_d, DI, DM)
    barrier()

    def qd_stream(src_d, qa_d, qb_d, nrows, K):
        c32.reset()
        c16.reset()
        cbf.reset()
        sc32, scbf = qd_scratch(K)
        wf = [c32.take(K), c32.take(K)]
        qa = [c16.take(K), c16.take(K)]
        qb = [c16.take(K), c16.take(K)]
        pv = [cnt["V"], cnt["V"]]
        pst = [0, 0]
        for i in range(nrows // 128):
            p = i % 2
            rows = slice(i * 128, (i + 1) * 128)
            la = dma("sync", wf[p], src_d[rows, :], sem="LA", waits=[("V", pv[p])])
            v = emit_qd(wf[p], K, qa[p], qb[p], sc32, scbf,
                        waits=[("LA", la), ("SG", pst[p])])
            dma("gpsimd", qa_d[rows, :], qa[p], sem="SG", waits=[("V", v)])
            pst[p] = dma("gpsimd", qb_d[rows, :], qb[p], sem="SG", waits=[("V", v)])
            pv[p] = v

    qd_stream(wg_d, qwga_d, qwgb_d, NF, DM)
    qd_stream(wu_d, qwua_d, qwub_d, NF, DM)
    barrier()
    if stage == "p1":
        return finish(nc, ops, sems, cnt)

    c32.reset()
    c16.reset()
    cbf.reset()
    xn_t = [c32.take(DM) for _ in range(16)]
    mu = c32.take(1)
    ssum = c32.take(1)
    rvs = c32.take(1)
    rstd = c32.take(1)
    sqbuf = c32.take(DM)
    ln_done = []
    for i in range(16):
        rows = slice(i * 128, (i + 1) * 128)
        la = dma("sync", xn_t[i], x_d[rows, :], sem="LA")
        v1 = step("vector", "V", [("LA", la)], [
            lambda e, _x=xn_t[i]: e.reduce_sum(mu, _x, AX.X),
            lambda e: (e.drain(), e.tensor_scalar(mu, mu, 1.0 / DM, None, ALU.mult))[1],
            lambda e, _x=xn_t[i]: (e.drain(), e.tensor_scalar(_x, _x, mu, None, ALU.subtract))[1],
        ])
        s1 = step("scalar", "S", [("V", v1)], [
            lambda e, _x=xn_t[i]: e.activation(sqbuf, _x, AF.Square),
        ])
        v2 = step("vector", "V", [("S", s1)], [
            lambda e: e.reduce_sum(ssum, sqbuf, AX.X),
            lambda e: (e.drain(), e.tensor_scalar(rvs, ssum, 1.0 / DM, 1e-5, ALU.mult, ALU.add))[1],
            lambda e: (e.drain(), e.reciprocal(rvs, rvs))[1],
        ])
        s2 = step("scalar", "S", [("V", v2)], [
            lambda e: e.activation(rstd, rvs, AF.Sqrt),
        ])
        v3 = step("vector", "V", [("S", s2)], [
            lambda e, _x=xn_t[i]: (e.drain(), e.tensor_scalar(_x, _x, rstd, None, ALU.mult))[1],
        ])
        ln_done.append(v3)

    sc32, scbf = qd_scratch(DM)
    xh = c16.take(DM)
    xl = c16.take(DM)
    qa = c16.take(DM)
    qb = c16.take(DM)
    pst = 0
    for i in range(16):
        rows = slice(i * 128, (i + 1) * 128)
        v1 = step("vector", "V", [("V", ln_done[i]), ("ST", pst)], [
            lambda e, _x=xn_t[i]: e.tensor_copy(xh, _x),
            lambda e, _x=xn_t[i]: e.tensor_tensor(xl, _x, xh, ALU.subtract),
        ])
        dma("sync", xnh_d[rows, :], xh, sem="ST", waits=[("V", v1)])
        dma("sync", xnl_d[rows, :], xl, sem="ST", waits=[("V", v1)])
        v2 = emit_qd(xn_t[i], DM, qa, qb, sc32, scbf, waits=[("V", v1)])
        dma("sync", qxa_d[rows, :], qa, sem="ST", waits=[("V", v2)])
        pst = dma("sync", qxb_d[rows, :], qb, sem="ST", waits=[("V", v2)])
    barrier()
    if stage == "p2":
        return finish(nc, ops, sems, cnt)

    NOFFS = [0, 512, 1024, 1536, 2048, 2560]
    NSIZES = [512, 512, 512, 512, 512, 256]
    c32.reset()
    c16.reset()
    cbf.reset()
    h_t = [c32.take(NF) for _ in range(4)]
    sg = [c32.take(512) for _ in range(4)]
    p3a_sc32 = {"w0": c32.take(NF), "vv": c32.take(NF),
                "bm": c32.take(NF // 16), "sc": c32.take(NF // 16),
                "rs": c32.take(NF // 16)}
    qh_a = c32.take(NF // 2).bitcast(F16)
    qh_b = c32.take(NF // 2).bitcast(F16)
    p3a_scbf = {"shb": c32.take(NF // 32 + 8).bitcast(BF16),
                "smb": c32.take(NF // 32 + 8).bitcast(BF16)}
    for _mi in range(4):
        p3a_scbf[f"m{_mi}"] = c32.take(NF // 2).bitcast(BF16)
    qxTa = [c16.take(512) for _ in range(KC)]
    qxTb = [c16.take(512) for _ in range(KC)]
    qwt0 = {nm: [c16.take(512) for _ in range(KC)]
            for nm in ("ga", "gb", "ua", "ub")}
    abf16 = ABF.bitcast(F16)
    qwt1 = {nm: [abf16[:, (j * KC + k) * 512:(j * KC + k + 1) * 512]
                 for k in range(KC)]
            for j, nm in enumerate(("ga", "gb", "ua", "ub"))}
    qwts = [qwt0, qwt1]
    mm_done_par = [0, 0]
    h_evac = [0, 0, 0, 0]
    p3a_pst = [0]
    wdw = [h_t[0][:, :NF], h_t[1][:, :NF]]
    wd_pv = [cnt["V"], cnt["V"]]
    wd_pst = 0
    for i in range(DM // 128):
        p = i % 2
        rows = slice(i * 128, (i + 1) * 128)
        la = dma("scalar", wdw[p], wd_d[rows, :], sem="LB",
                 waits=[("V", wd_pv[p])])
        v = emit_qd(wdw[p], NF, qh_a, qh_b, p3a_sc32, p3a_scbf,
                    waits=[("LB", la), ("SG", wd_pst)])
        dma("gpsimd", qwda_d[rows, :], qh_a, sem="SG", waits=[("V", v)])
        wd_pst = dma("gpsimd", qwdb_d[rows, :], qh_b, sem="SG", waits=[("V", v)])
        wd_pv[p] = v
    p3a_pst[0] = wd_pst
    for tb in range(4):
        t0 = tb * 512
        lx = 0
        for k in range(KC):
            w = [("ST", cnt["ST"]), ("T", cnt["T"])] if k == 0 else []
            lx = dma("sync", qxTa[k], qxa_d[t0:t0 + 512, k * 128:(k + 1) * 128],
                     sem="LA", waits=w, transpose=True)
            lx = dma("sync", qxTb[k], qxb_d[t0:t0 + 512, k * 128:(k + 1) * 128],
                     sem="LA", transpose=True)
        for ni, (n0, nsz) in enumerate(zip(NOFFS, NSIZES)):
            par = ni % 2
            qwt = qwts[par]
            lw = 0
            for k in range(KC):
                kcols = slice(k * 128, (k + 1) * 128)
                nrows = slice(n0, n0 + nsz)
                w = [("T", mm_done_par[par])] if k == 0 else []
                lw = dma("sync", qwt["ga"][k][:, :nsz], qwga_d[nrows, kcols],
                         sem="LA", waits=w, transpose=True)
                lw = dma("sync", qwt["gb"][k][:, :nsz], qwgb_d[nrows, kcols],
                         sem="LA", transpose=True)
                lw = dma("sync", qwt["ua"][k][:, :nsz], qwua_d[nrows, kcols],
                         sem="LA", transpose=True)
                lw = dma("sync", qwt["ub"][k][:, :nsz], qwub_d[nrows, kcols],
                         sem="LA", transpose=True)
            for tt in range(4):
                bg, bu = (tt * 2) % 8, (tt * 2 + 1) % 8
                mm = []
                for k in range(KC):
                    sa = qxTa[k][:, tt * 128:(tt + 1) * 128]
                    sb = qxTb[k][:, tt * 128:(tt + 1) * 128]
                    fst, lst = (k == 0), (k == KC - 1)
                    mm.append(lambda e, _s=sa, _r=qwt["ga"][k][:, :nsz], _b=bg, _f=fst, _n=nsz:
                              e.matmul(psum[_b][:, :_n], _s, _r, start=_f, stop=False))
                    mm.append(lambda e, _s=sa, _r=qwt["ua"][k][:, :nsz], _b=bu, _f=fst, _n=nsz:
                              e.matmul(psum[_b][:, :_n], _s, _r, start=_f, stop=False))
                    mm.append(lambda e, _s=sa, _r=qwt["gb"][k][:, :nsz], _b=bg, _n=nsz:
                              e.matmul(psum[_b][:, :_n], _s, _r, start=False, stop=False))
                    mm.append(lambda e, _s=sa, _r=qwt["ub"][k][:, :nsz], _b=bu, _n=nsz:
                              e.matmul(psum[_b][:, :_n], _s, _r, start=False, stop=False))
                    mm.append(lambda e, _s=sb, _r=qwt["ga"][k][:, :nsz], _b=bg, _l=lst, _n=nsz:
                              e.matmul(psum[_b][:, :_n], _s, _r, start=False, stop=_l))
                    mm.append(lambda e, _s=sb, _r=qwt["ua"][k][:, :nsz], _b=bu, _l=lst, _n=nsz:
                              e.matmul(psum[_b][:, :_n], _s, _r, start=False, stop=_l))
                tmm = step("tensor", "T", [("LA", lw),
                                           bank_free[bg], bank_free[bu]], mm)
                mm_done_par[par] = tmm
                se = step("scalar", "S", [("T", tmm), ("V", h_evac[tt])], [
                    lambda e, _b=bg, _n=nsz, _g=sg[tt]: e.activation(_g[:, :_n], psum[_b][:, :_n], AF.Silu),
                ])
                ve = step("vector", "V", [("S", se)], [
                    lambda e, _b=bu, _h=h_t[tt][:, n0:n0 + nsz], _n=nsz, _g=sg[tt]:
                        e.tensor_tensor(_h, psum[_b][:, :_n], _g[:, :_n], ALU.mult),
                ])
                h_evac[tt] = ve
                bank_free[bg] = ("S", se)
                bank_free[bu] = ("V", ve)
        for tt in range(4):
            rows = slice(t0 + tt * 128, t0 + (tt + 1) * 128)
            v = emit_qd(h_t[tt], NF, qh_a, qh_b, p3a_sc32, p3a_scbf,
                        waits=[("V", h_evac[tt]), ("SG", p3a_pst[0])])
            dma("gpsimd", qha_d[rows, :], qh_a, sem="SG", waits=[("V", v)])
            p3a_pst[0] = dma("gpsimd", qhb_d[rows, :], qh_b, sem="SG", waits=[("V", v)])
            h_evac[tt] = v
    barrier()
    if stage == "p3a":
        return finish(nc, ops, sems, cnt)

    c32.reset()
    c16.reset()
    cbf.reset()
    xbr = c32.take(DIC * 515)
    xcy = c32.take(DIC * 512)
    mmr = c32.take(DIC * 512)
    ssm_s = c32.take(512)
    dtb = c32.take(512)
    sp1 = c32.take(512)
    sp2 = c32.take(512)
    dAb = [c32.take(512), c32.take(512)]
    ub = [c32.take(512), c32.take(512)]
    hbf2 = [c32.take(512), c32.take(512)]
    tmpb = sp2
    Bsb = [c32.take(512), c32.take(512)]
    Csb = [dtb, ssm_s]
    szb = sp1

    xnTh = [c16.take(512) for _ in range(KC)]
    xnTl = [c16.take(512) for _ in range(KC)]
    wstah = [c16.take(2048), c16.take(2048)]
    wstal = [c16.take(2048), c16.take(2048)]
    y2h = c16.take(DIC * 512)
    y2l = ABF.bitcast(F16)[:, 0:DIC * 512]
    xchb = y2h[:, 0:512]
    xclb = y2h[:, 512:1024]

    KD = NF // 128
    abfv = ABF.bitcast(F16)
    p3q = [abfv[:, 8192 + sl * 3072: 8192 + (sl + 1) * 3072] for sl in range(2)]
    p3stg = [ABF.bitcast(F32)[:, 7168 + j * 512: 7168 + (j + 1) * 512]
             for j in range(2)]
    p3_chain_T = 0
    p3_st = [0, 0]

    def xb_t(i):
        return xbr[:, i * 515:(i + 1) * 515]

    def dt_t(i):
        return xbr[:, i * 512:(i + 1) * 512]

    def xc_t(i):
        return xcy[:, i * 512:(i + 1) * 512]

    def m_t(i):
        return mmr[:, i * 512:(i + 1) * 512]

    sum_t = [mmr[:, i * 512:(i + 1) * 512] for i in range(8)]
    outtm = mmr[:, 8 * 512:8 * 512 + 1024]
    xtm = mmr[:, 8 * 512 + 1024:8 * 512 + 2048]

    def load_sta(dst, src2d, kchunks, w):
        return dma("gpsimd",
                   dst[:, :kchunks * 128].rearrange("p (k n) -> p k n", n=128),
                   src2d.rearrange("(k p) n -> p k n", p=128),
                   sem="LG", waits=w)

    for ch in range(NCH):
        tsl = slice(ch * 512, (ch + 1) * 512)
        lxn = 0
        for k in range(KC):
            kcols = slice(k * 128, (k + 1) * 128)
            w = [("ST", cnt["ST"]), ("SG", cnt["SG"]), ("T", cnt["T"])] if k == 0 else []
            lxn = dma("sync", xnTh[k], xnh_d[tsl, kcols], sem="LA",
                      waits=w, transpose=True)
            lxn = dma("sync", xnTl[k], xnl_d[tsl, kcols], sem="LA",
                      transpose=True)
        s_cv = []
        win_mm_par = [0, 0]
        for n in range(DIC):
            par = n % 2
            ncols = slice(n * 128, (n + 1) * 128)
            load_sta(wstah[par], winh_d[:, ncols], KC, [("T", win_mm_par[par])])
            lw = load_sta(wstal[par], winl_d[:, ncols], KC, [])
            bnk = n % 4
            mm = []
            for k in range(KC):
                fst, lst = (k == 0), (k == KC - 1)
                wh = wstah[par][:, k * 128:(k + 1) * 128]
                wl = wstal[par][:, k * 128:(k + 1) * 128]
                mm.append(lambda e, _w=wh, _r=xnTh[k], _b=bnk, _f=fst:
                          e.matmul(psum[_b], _w, _r, start=_f, stop=False))
                mm.append(lambda e, _w=wh, _r=xnTl[k], _b=bnk:
                          e.matmul(psum[_b], _w, _r, start=False, stop=False))
                mm.append(lambda e, _w=wl, _r=xnTh[k], _b=bnk, _l=lst:
                          e.matmul(psum[_b], _w, _r, start=False, stop=_l))
            tmm = step("tensor", "T", [("LG", lw), ("LA", lxn), bank_free[bnk]], mm)
            win_mm_par[par] = tmm
            s_ev = step("scalar", "S", [("T", tmm)], [
                lambda e, _b=bnk, _x=xb_t(n): e.activation(_x[:, 3:515], psum[_b], AF.Copy),
            ])
            bank_free[bnk] = ("S", s_ev)
            halo3 = halo_s[:, n * 3:(n + 1) * 3]
            if ch == 0:
                pre = [lambda e, _x=xb_t(n): e.memset(_x[:, 0:3], 0.0)]
            else:
                pre = [lambda e, _x=xb_t(n), _h=halo3: e.tensor_copy(_x[:, 0:3], _h)]
            vcv = step("vector", "V", [("S", s_ev)], pre + [
                lambda e, _x=xb_t(n), _c=cw_s[:, n * 4:n * 4 + 1]:
                    e.tensor_scalar(sp1, _x[:, 0:512], _c, None, ALU.mult),
                lambda e, _x=xb_t(n), _c=cw_s[:, n * 4 + 1:n * 4 + 2]:
                    e.scalar_tensor_tensor(sp2, _x[:, 1:513], _c, sp1, ALU.mult, ALU.add),
                lambda e, _x=xb_t(n), _c=cw_s[:, n * 4 + 2:n * 4 + 3]:
                    e.scalar_tensor_tensor(sp1, _x[:, 2:514], _c, sp2, ALU.mult, ALU.add),
                lambda e, _x=xb_t(n), _c=cw_s[:, n * 4 + 3:n * 4 + 4]:
                    e.scalar_tensor_tensor(sp2, _x[:, 3:515], _c, sp1, ALU.mult, ALU.add),
                lambda e, _x=xb_t(n), _h=halo3: e.tensor_copy(_h, _x[:, 512:515]),
            ])
            sc_ = step("scalar", "S", [("V", vcv)], [
                lambda e, _o=xc_t(n), _b=cb_s[:, n:n + 1]:
                    e.activation(_o, sp2, AF.Silu, bias=_b),
            ])
            s_cv.append(sc_)
            if "s_xb" in dbg:
                dma("sync", xb_dbg[n * 128:(n + 1) * 128, tsl], xb_t(n)[:, 3:515],
                    sem="ST", waits=[("V", vcv)])
            if "s_xc" in dbg:
                dma("sync", xc_dbg[n * 128:(n + 1) * 128, tsl], xc_t(n),
                    sem="ST", waits=[("S", sc_)])
        prev_wx_mm = 0
        for k in range(DIC):
            vch = step("vector", "V", [("S", s_cv[k]), ("T", prev_wx_mm)], [
                lambda e, _x=xc_t(k): e.tensor_copy(xchb, _x),
                lambda e, _x=xc_t(k): e.tensor_tensor(xclb, _x, xchb, ALU.subtract),
            ])
            fst, lst = (k == 0), (k == DIC - 1)
            wcols = slice(k * 33, (k + 1) * 33)
            mm = [
                lambda e, _w=wxh_s[:, wcols], _f=fst:
                    e.matmul(psum[4][0:33, :], _w, xchb, start=_f, stop=False),
                lambda e, _w=wxh_s[:, wcols]:
                    e.matmul(psum[4][0:33, :], _w, xclb, start=False, stop=False),
                lambda e, _w=wxl_s[:, wcols], _l=lst:
                    e.matmul(psum[4][0:33, :], _w, xchb, start=False, stop=_l),
            ]
            wts = [("V", vch)]
            if k == 0:
                wts.append(bank_free[4])
            tmm = step("tensor", "T", wts, mm)
            prev_wx_mm = tmm
        s_sx = step("scalar", "S", [("T", cnt["T"])], [
            lambda e: e.activation(ssm_s[0:33, :], psum[4][0:33, :], AF.Copy),
        ])
        bank_free[4] = ("S", s_sx)
        st_ssm = dma("scalar", ssm_d[:, tsl], ssm_s[0:33, :], sem="SC",
                     waits=[("S", s_sx)])
        p3_ck = {}
        pb_free = [bank_free[d] for d in range(8)]
        for k in range(KD):
            kcols = slice(k * 128, (k + 1) * 128)
            sl = p3q[k % 2]
            if k < 2:
                w = [("T", p3_chain_T)]
            else:
                w = [("T", p3_ck[k - 2])]
            la = dma("sync", sl[:, 0:512], qha_d[tsl, kcols],
                     sem="LA", waits=w, transpose=True)
            la = dma("sync", sl[:, 512:1024], qhb_d[tsl, kcols],
                     sem="LA", transpose=True)
            la = dma("sync", sl[:, 1024:2048], qwda_d[:, kcols],
                     sem="LA", transpose=True)
            la = dma("sync", sl[:, 2048:3072], qwdb_d[:, kcols],
                     sem="LA", transpose=True)
            mm = []
            for d in range(8):
                fst, lst = (k == 0), (k == KD - 1)
                qa, qb = sl[:, 0:512], sl[:, 512:1024]
                wa = sl[:, 1024 + d * 128:1024 + (d + 1) * 128]
                wb = sl[:, 2048 + d * 128:2048 + (d + 1) * 128]
                mm.append(lambda e, _w=wa, _r=qa, _b=d, _f=fst:
                          e.matmul(psum[_b], _w, _r, start=_f, stop=False))
                mm.append(lambda e, _w=wa, _r=qb, _b=d:
                          e.matmul(psum[_b], _w, _r, start=False, stop=False))
                mm.append(lambda e, _w=wb, _r=qa, _b=d, _l=lst:
                          e.matmul(psum[_b], _w, _r, start=False, stop=_l))
            wt = [("LA", la)]
            if k == 0:
                wt += pb_free
            p3_ck[k] = step("tensor", "T", wt, mm)
        p3_chain_T = p3_ck[KD - 1]
        l_dtb = dma("scalar", dtb, ssm_d[32:33, tsl].partition_broadcast(128),
                    sem="SC", waits=[("SC", st_ssm)])
        vdt_prev = 0
        for i in range(DIC):
            sdt = step("scalar", "S", [("SC", l_dtb), ("V", vdt_prev)], [
                lambda e, _b=db_s[:, i:i + 1]: e.activation(sp1, dtb, AF.Abs, bias=_b),
                lambda e: e.activation(sp2, sp1, AF.Exp, scale=csts(IC_NEG1)),
                lambda e: e.activation(sp1, sp2, AF.Ln, bias=csts(IC_1)),
            ])
            vdt_prev = step("vector", "V", [("S", sdt)], [
                lambda e, _b=db_s[:, i:i + 1]:
                    e.tensor_scalar(sp2, dtb, _b, 0.0, ALU.add, ALU.max),
                lambda e, _d=dt_t(i): e.tensor_tensor(_d, sp1, sp2, ALU.add),
            ])
        if "s_dt" in dbg:
            for i in range(DIC):
                dma("sync", dt_dbg[i * 128:(i + 1) * 128, tsl], dt_t(i),
                    sem="ST", waits=[("V", cnt["V"])])
        for i in range(DIC):
            step("vector", "V", [], [
                lambda e, _m=m_t(i), _d=dt_t(i), _x=xc_t(i):
                    e.tensor_tensor(_m, _d, _x, ALU.mult),
                lambda e, _x=xc_t(i), _p=dp_s[:, i:i + 1]:
                    e.tensor_scalar(_x, _x, _p, None, ALU.mult),
            ])
        VB, SB, GB = cnt["V"], cnt["S"], cnt["G"]
        bs_val = {}
        cs_val = {}

        def emit_bcs(s):
            w = [] if s < 2 else [("V", VB + (s - 1) * DIC)]
            wc = [("V", VB), ("SC", st_ssm)] if s < 2 else w
            bs_val[s] = dma("scalar", Bsb[s % 2],
                            ssm_d[s:s + 1, tsl].partition_broadcast(128),
                            sem="SC", waits=w)
            cs_val[s] = dma("scalar", Csb[s % 2],
                            ssm_d[S + s:S + s + 1, tsl].partition_broadcast(128),
                            sem="SC", waits=wc)

        emit_bcs(0)
        emit_bcs(1)
        prev_col = None
        for s in range(S):
            for ci in range(DIC):
                i = s * DIC + ci
                col = ci * S + s
                step("scalar", "S", [("V", max(VB, VB + i - 1))], [
                    lambda e, _o=dAb[i % 2], _d=dt_t(ci), _a=a_s[:, col:col + 1]:
                        e.activation(_o, _d, AF.Exp, scale=_a),
                ])
                if ci == DIC - 1 and s + 2 < S:
                    emit_bcs(s + 2)
                step("gpsimd", "G",
                     [("V", max(VB, VB + i - 1)), ("SC", bs_val[s])], [
                    lambda e, _o=ub[i % 2], _m=m_t(ci), _b=Bsb[s % 2]:
                        e.tensor_tensor(_o, _m, _b, ALU.mult),
                ])
                bl = [
                    lambda e, _a=dAb[i % 2], _u=ub[i % 2], _o=hbf2[i % 2],
                           _h=hcar_s[:, col:col + 1]:
                        e.tensor_tensor_scan(_o, _a, _u, _h, ALU.mult, ALU.add),
                ]
                if prev_col is not None:
                    bl.append(lambda e, _h=hcar_s[:, prev_col:prev_col + 1],
                              _s=hbf2[(i - 1) % 2]:
                              e.tensor_copy(_h, _s[:, 511:512]))
                bl += [
                    lambda e, _c=Csb[s % 2], _o=hbf2[i % 2]:
                        e.tensor_tensor(tmpb, _o, _c, ALU.mult),
                    lambda e, _y=xc_t(ci):
                        e.tensor_tensor(_y, _y, tmpb, ALU.add),
                ]
                step("vector", "V",
                     [("S", SB + i + 1), ("G", GB + i + 1), ("SC", cs_val[s])], bl)
                prev_col = col
        step("vector", "V", [], [
            lambda e, _h=hcar_s[:, prev_col:prev_col + 1],
                   _s=hbf2[(S * DIC - 1) % 2]:
                (e.drain(), e.tensor_copy(_h, _s[:, 511:512]))[1],
        ])
        for d in range(8):
            j = d % 2
            ev = step("scalar", "S", [("T", p3_chain_T), ("ST", p3_st[j])], [
                lambda e, _b=d, _s=p3stg[j]: e.activation(_s, psum[_b], AF.Copy),
            ])
            p3_st[j] = dma("sync", ffn_d[d * 128:(d + 1) * 128, tsl], p3stg[j],
                           sem="ST", waits=[("S", ev)])
            bank_free[d] = ("S", ev)
        p3_ffn_st = max(p3_st)
        if "s_y" in dbg:
            for i in range(DIC):
                dma("sync", y_dbg[i * 128:(i + 1) * 128, tsl], xc_t(i),
                    sem="ST", waits=[("V", cnt["V"])])
        v_y2p = 0
        z_mm_par = [cnt["T"], cnt["T"]]
        for i in range(DIC):
            par = i % 2
            zcols = slice(DI + i * 128, DI + (i + 1) * 128)
            load_sta(wstah[par], winh_d[:, zcols], KC, [("T", z_mm_par[par])])
            lw = load_sta(wstal[par], winl_d[:, zcols], KC, [])
            bnk = i % 4
            mm = []
            for k in range(KC):
                fst, lst = (k == 0), (k == KC - 1)
                wh = wstah[par][:, k * 128:(k + 1) * 128]
                wl = wstal[par][:, k * 128:(k + 1) * 128]
                mm.append(lambda e, _w=wh, _r=xnTh[k], _b=bnk, _f=fst:
                          e.matmul(psum[_b], _w, _r, start=_f, stop=False))
                mm.append(lambda e, _w=wh, _r=xnTl[k], _b=bnk:
                          e.matmul(psum[_b], _w, _r, start=False, stop=False))
                mm.append(lambda e, _w=wl, _r=xnTh[k], _b=bnk, _l=lst:
                          e.matmul(psum[_b], _w, _r, start=False, stop=_l))
            tmm = step("tensor", "T", [("LG", lw), bank_free[bnk]], mm)
            z_mm_par[par] = tmm
            s_z = step("scalar", "S", [("T", tmm), ("V", v_y2p)], [
                lambda e, _b=bnk: e.activation(szb, psum[_b], AF.Silu),
            ])
            bank_free[bnk] = ("S", s_z)
            v_y2p = step("vector", "V", [("S", s_z)], [
                lambda e, _y=xc_t(i): e.tensor_tensor(_y, _y, szb, ALU.mult),
                lambda e, _y=xc_t(i), _o=y2h[:, i * 512:(i + 1) * 512]:
                    e.tensor_copy(_o, _y),
                lambda e, _y=xc_t(i), _h=y2h[:, i * 512:(i + 1) * 512],
                       _o=y2l[:, i * 512:(i + 1) * 512]:
                    e.tensor_tensor(_o, _y, _h, ALU.subtract),
            ])
        v_wo = [0] * 8
        wo_mm_par = [cnt["T"], cnt["T"]]
        for no in range(8):
            par = no % 2
            ncols = slice(no * 128, (no + 1) * 128)
            load_sta(wstah[par], wouth_d[:, ncols], DIC, [("T", wo_mm_par[par])])
            lw = load_sta(wstal[par], woutl_d[:, ncols], DIC, [])
            fl = dma("gpsimd", dAb[no % 2], ffn_d[ncols, tsl], sem="LG",
                     waits=[("V", v_wo[no - 2] if no >= 2 else cnt["V"]),
                            ("ST", p3_ffn_st)])
            bnk = 4 + no % 4
            mm = []
            for k in range(DIC):
                fst, lst = (k == 0), (k == DIC - 1)
                wh = wstah[par][:, k * 128:(k + 1) * 128]
                wl = wstal[par][:, k * 128:(k + 1) * 128]
                y2hs = y2h[:, k * 512:(k + 1) * 512]
                y2ls = y2l[:, k * 512:(k + 1) * 512]
                mm.append(lambda e, _w=wh, _r=y2hs, _b=bnk, _f=fst:
                          e.matmul(psum[_b], _w, _r, start=_f, stop=False))
                mm.append(lambda e, _w=wh, _r=y2ls, _b=bnk:
                          e.matmul(psum[_b], _w, _r, start=False, stop=False))
                mm.append(lambda e, _w=wl, _r=y2hs, _b=bnk, _l=lst:
                          e.matmul(psum[_b], _w, _r, start=False, stop=_l))
            tmm = step("tensor", "T", [("LG", max(lw, fl)), ("V", v_y2p), bank_free[bnk]], mm)
            wo_mm_par[par] = tmm
            v_wo[no] = step("vector", "V", [("T", tmm), ("LG", fl)], [
                lambda e, _b=bnk, _s=sum_t[no], _f=dAb[no % 2]:
                    e.tensor_tensor(_s, psum[_b], _f, ALU.add),
            ])
            bank_free[bnk] = ("V", v_wo[no])
        st_out = 0
        v_as = 0
        for tsub in range(4):
            rows = slice(ch * 512 + tsub * 128, ch * 512 + (tsub + 1) * 128)
            lxr = dma("gpsimd", xtm, x_d[rows, :], sem="LG",
                      waits=[("V", v_as)])
            for no in range(8):
                bnk = no % 2
                ttr = step("tensor", "T",
                           [bank_free[bnk], ("V", v_wo[no])], [
                    lambda e, _s=sum_t[no][:, tsub * 128:(tsub + 1) * 128], _b=bnk:
                        e.transpose(psum[_b][:, 0:128], _s, idn_s),
                ])
                v_as = step("vector", "V",
                            [("T", ttr), ("LG", lxr), ("SG", st_out)], [
                    lambda e, _b=bnk, _o=outtm[:, no * 128:(no + 1) * 128],
                           _x=xtm[:, no * 128:(no + 1) * 128]:
                        e.tensor_tensor(_o, psum[_b][:, 0:128], _x, ALU.add),
                ])
                bank_free[bnk] = ("V", v_as)
            st_out = dma("gpsimd", out_d[rows, :], outtm, sem="SG",
                         waits=[("V", v_as)])
        barrier()

    return finish(nc, ops, sems, cnt)


def finish(nc, ops, sems, cnt):
    import contextlib

    final = [(s, cnt[s]) for s in ("ST", "SG", "SC", "LA", "LB", "LG") if cnt[s] > 0]

    def tail(e, _f=tuple(final)):
        for s2, val in _f:
            e.wait_ge(sems[s2], val)
    ops["sync"].append(tail)

    with contextlib.ExitStack() as st:
        for n in ["LA", "LB", "LG", "T", "V", "S", "G", "ST", "SG", "SC"]:
            sems[n] = st.enter_context(nc.semaphore(n))
        block = st.enter_context(nc.Block())
        for eng in ENGS:
            fns = tuple(ops[eng])

            def body(e, _fns=fns):
                for f in _fns:
                    f(e)
            getattr(block, eng)(body)
    return nc


def _prep_maps(inputs):
    f32 = np.float32
    x = np.ascontiguousarray(inputs["L0_attn_x"], dtype=f32)
    win = np.ascontiguousarray(inputs["L0_attn_W_in"], dtype=f32)
    wout = np.ascontiguousarray(inputs["L0_attn_W_out"], dtype=f32)
    wg = np.ascontiguousarray(inputs["L0_ffn_w_gate"], dtype=f32)
    wu = np.ascontiguousarray(inputs["L0_ffn_w_up"], dtype=f32)
    wd = np.ascontiguousarray(inputs["L0_ffn_w_down"], dtype=f32)
    alog = np.ascontiguousarray(
        np.asarray(inputs["L0_attn_A_log"], dtype=f32)
        .reshape(DIC, 128, S).transpose(1, 0, 2).reshape(128, DIC * S))
    db = np.ascontiguousarray(
        np.asarray(inputs["L0_attn_dt_bias"], dtype=f32).reshape(DIC, 128).T)
    dp = np.ascontiguousarray(
        np.asarray(inputs["L0_attn_D_param"], dtype=f32).reshape(DIC, 128).T)
    cb = np.ascontiguousarray(
        np.asarray(inputs["L0_attn_conv_b"], dtype=f32).reshape(DIC, 128).T)
    cw = np.ascontiguousarray(
        np.asarray(inputs["L0_attn_conv_w"], dtype=f32)[:, 0, :]
        .reshape(DIC, 128, 4).transpose(1, 0, 2).reshape(128, DIC * 4))
    wx = np.ascontiguousarray(
        np.asarray(inputs["L0_attn_W_x"], dtype=f32)
        .reshape(DIC, 128, 33).transpose(1, 0, 2).reshape(128, DIC * 33))
    cst = np.tile(np.asarray(CST, dtype=f32)[None, :], (128, 1))
    idn = np.eye(128, dtype=f32)
    shared = dict(w_in=win, w_out=wout, w_gate=wg, w_up=wu, w_down=wd,
                  a_log=alog, dt_bias=db, d_param=dp, conv_b=cb, conv_w=cw,
                  w_x=wx, cst=cst, idn=idn)
    maps = []
    for i in range(8):
        m = dict(shared)
        m["x"] = np.ascontiguousarray(x[i])
        maps.append(m)
    return maps


def kernel(**inputs):
    nc = build()
    maps = _prep_maps(inputs)
    res = run_bass_kernel_spmd(nc, maps, core_ids=list(range(8)))
    out = np.stack([np.asarray(r["out"]) for r in res.results], axis=0)
    return out.astype(np.float32)
